# version 13
# baseline (speedup 1.0000x reference)
import numpy as np
import ml_dtypes
import contextlib
import concourse.bass as bass
import concourse.mybir as mybir
from concourse.bass_utils import run_bass_kernel_spmd
from concourse.alu_op_type import AluOpType as ALU
import concourse.bass_utils as _bu

if not getattr(_bu, "_ldw_opt_patched", False):
    _orig_run_command = _bu.run_command

    def _run_command_ldw(cmd, *a, **kw):
        cmd = [("--enable-ldw-opt=true" if c == "--enable-ldw-opt=false" else c) for c in cmd]
        return _orig_run_command(cmd, *a, **kw)

    _bu.run_command = _run_command_ldw
    _bu._ldw_opt_patched = True

F32 = mybir.dt.float32
BF16 = mybir.dt.bfloat16
I8 = mybir.dt.int8
U16 = mybir.dt.uint16
AF = mybir.ActivationFunctionType
AX = mybir.AxisListType

L = 4096
NCORE = 8
TOKC = 1024
CH = 512
NCH = TOKC // CH
NIB = 86
IGROUPS = [(g * 4, 4) for g in range(21)] + [(84, 2)]
PAD = 512
NSLOT = 9
ENGINES = ("sync", "gpsimd", "vector", "scalar", "tensor")


class Prog:

    def __init__(self, nc, es):
        self.nc = nc
        self.stream = {e: [] for e in ENGINES}
        self.sems = {}
        self.es = es

    def sem(self, name):
        h = self.es.enter_context(self.nc.semaphore(name))
        s = {"h": h, "n": 0, "name": name}
        self.sems[name] = s
        return s

    def op(self, eng, fn, waits=(), sem=None, dma=False):
        waits = [w for w in waits if w is not None]
        if eng in ("vector", "scalar") and not dma:
            chain = self.sems[eng.upper() + "CH"]
            if chain["n"] > 0:
                waits.append((chain, chain["n"]))
            assert sem is None or sem is chain
            sem = chain
            sem["n"] += 1
            self.stream[eng].append((fn, waits, sem, 1))
            return (sem, sem["n"])
        tok = None
        if sem is not None:
            sem["n"] += 16 if dma else 1
            tok = (sem, sem["n"])
        self.stream[eng].append((fn, waits, sem, 16 if dma else 1))
        return tok

    def replay(self, block):
        for eng in ENGINES:
            stream = self.stream[eng]
            if not stream:
                continue

            def make(stream):
                def body(h):
                    hw = {}
                    for fn, waits, sem, inc in stream:
                        for s, v in waits:
                            if hw.get(id(s), -1) < v:
                                h.wait_ge(s["h"], v)
                                hw[id(s)] = v
                        ins = fn(h)
                        if sem is not None:
                            ins.then_inc(sem["h"], inc)
                return body
            getattr(block, eng)(make(stream))


def host_tables():
    f32, f64 = np.float32, np.float64
    k = np.arange(L)
    x = (k / L).astype(f32)
    th = (x * np.float32(-2 * np.pi)).astype(f32)
    O = np.cos(th.astype(f64)).astype(f32).astype(f64) \
        + 1j * np.sin(th.astype(f64)).astype(f32).astype(f64)
    c = 2.0 / (1.0 + O)
    t = (1.0 - O) / (1.0 + O)
    n = np.arange(64)
    E = np.exp(2j * np.pi * np.outer(n, n) / 64.0)
    tw = np.exp(2j * np.pi * np.outer(n, n) / float(L))
    R = np.zeros((128, 128), f32)
    R[np.arange(128), 127 - np.arange(128)] = 1.0
    return {
        "c_re": np.ascontiguousarray(c.real.astype(f32).reshape(64, 64)),
        "c_im": np.ascontiguousarray(c.imag.astype(f32).reshape(64, 64)),
        "tS": np.ascontiguousarray(np.concatenate(
            [np.broadcast_to(t.real.astype(f32), (64, L)),
             np.broadcast_to(t.imag.astype(f32), (64, L))], 0)),
        "Es1re": np.ascontiguousarray(np.concatenate([E.real, -E.imag], 0).astype(f32)),
        "Es1im": np.ascontiguousarray(np.concatenate([E.imag, E.real], 0).astype(f32)),
        "Es3": np.ascontiguousarray(np.concatenate([E.real / L, -E.imag / L], 0).astype(f32)),
        "tw_re": np.ascontiguousarray(tw.real.astype(f32)),
        "tw_im": np.ascontiguousarray(tw.imag.astype(f32)),
        "Rrev": R,
        "ident_f": np.eye(128, dtype=f32),
        "ones_row": np.ones((1, 128), f32),
        "exp_lo": (np.arange(76) / 16.0 - 6.9375 - 1.0 / 32).astype(f32).reshape(1, 76),
        "exp_h": (np.arange(76) / 16.0 - 6.9375).astype(f32).reshape(1, 76),
        "exp_e": np.exp(np.arange(76) / 16.0 - 6.9375).astype(f32).reshape(1, 76),
    }


def stage_weights(gate_wq, up_wq, down_wq):
    def i8(w):
        return np.ascontiguousarray(w).view(np.int8)[:, ::4]

    out = {}
    for name, wq in (("g", gate_wq), ("u", up_wq)):
        wT = i8(wq).T
        out[name + "main"] = np.ascontiguousarray(
            wT[:, :10752].reshape(8, 4, 128, 21, 512).transpose(3, 0, 2, 1, 4))
        out[name + "tail"] = np.ascontiguousarray(
            wT[:, 10752:].reshape(8, 4, 128, 256).transpose(0, 2, 1, 3))
    dT = i8(down_wq).T.astype(ml_dtypes.bfloat16).view(np.uint16)
    out["dmain"] = np.ascontiguousarray(
        dT[:10752, :].reshape(21, 4, 128, 8, 512).transpose(3, 0, 2, 1, 4))
    out["dtail"] = np.ascontiguousarray(
        dT[10752:, :].reshape(2, 128, 8, 512).transpose(2, 1, 0, 3))
    return out


def build_program():
    nc = bass.Bass("TRN2", target_bir_lowering=False)
    D = {}
    dram_spec = [
        ("x", [TOKC, L], F32),
        ("Lam", [64, 2], F32), ("pri", [64, 2], F32), ("qri", [64, 2], F32),
        ("Vc", [64, 128], F32), ("Ct", [64, 2], F32), ("Bp", [64, 1], F32),
        ("Dsc", [1, 1], F32), ("lstep", [1, 1], F32),
        ("gs", [86, 128], F32), ("us", [86, 128], F32), ("ds", [1, L], F32),
        ("gmain", [21, 8, 128, 4, 512], I8), ("gtail", [8, 128, 4, 256], I8),
        ("umain", [21, 8, 128, 4, 512], I8), ("utail", [8, 128, 4, 256], I8),
        ("dmain", [8, 21, 128, 4, 512], U16), ("dtail", [8, 128, 2, 512], U16),
        ("c_re", [64, 64], F32), ("c_im", [64, 64], F32), ("tS", [128, L], F32),
        ("Es1re", [128, 64], F32), ("Es1im", [128, 64], F32), ("Es3", [128, 64], F32),
        ("tw_re", [64, 64], F32), ("tw_im", [64, 64], F32),
        ("Rrev", [128, 128], F32), ("ident_f", [128, 128], F32),
        ("ones_row", [1, 128], F32),
        ("exp_lo", [1, 76], F32), ("exp_h", [1, 76], F32), ("exp_e", [1, 76], F32),
    ]
    for name, shape, dt in dram_spec:
        D[name] = nc.dram_tensor(name, shape, dt, kind="ExternalInput").ap()
    OUT = nc.dram_tensor("o", [TOKC, L], F32, kind="ExternalOutput").ap()
    kpad = nc.dram_tensor("kpad", [PAD + L], F32, kind="Internal").ap()
    striph = nc.dram_tensor("striph", [128, 32 * 512], BF16, kind="Internal").ap()
    kabh = nc.dram_tensor("kabh", [8, 64, 64], F32, kind="Internal").ap()

    es = contextlib.ExitStack()
    with es:
        sb = {}

        def S(name, shape, dt=F32):
            sb[name] = es.enter_context(nc.sbuf_tensor("sb_" + name, shape, dt))
            return sb[name]

        S("xnT", [128, 32 * CH], BF16)
        S("gu", [128, NIB * CH], BF16)
        S("arena", [128, NSLOT * 2048], BF16)
        S("dring", [128, 2 * 2048], BF16)
        S("ssm0", [128, 4 * CH], BF16)
        S("ssm1", [128, 4 * CH], BF16)
        S("s_bc", [128, L], BF16)
        S("tmp", [128, 2 * CH], F32)
        S("xp", [128, 2 * CH], F32)
        S("xq", [128, 2048], F32)
        S("xnb", [128, L], BF16)
        S("ident_bf", [128, 128], BF16)
        S("Rrev_bf", [128, 128], BF16)
        S("Rrev", [128, 128], F32)
        S("ones_row", [1, 128], F32)
        S("s2", [128, 88], F32)
        S("stats", [128, 48], F32)
        S("mv", [128, 2], F32)
        S("rstdT", [128, 4], F32)
        S("nmrT", [128, 4], F32)
        for nm, shp in [("Lam", [64, 2]), ("pri", [64, 2]), ("qri", [64, 2]),
                        ("Ct", [64, 2]), ("Bp", [64, 1]),
                        ("Dsc", [1, 1]), ("lstep", [1, 1]),
                        ("c_re", [64, 64]), ("c_im", [64, 64]),
                        ("Es1re", [128, 64]), ("Es1im", [128, 64]), ("Es3", [128, 64]),
                        ("tw_re", [64, 64]), ("tw_im", [64, 64]),
                        ("step2", [1, 1]), ("g2x", [128, 1]), ("lamS", [128, 1]), ("neglam", [128, 1]),
                        ("lhsTS", [128, 8]),
                        ]:
            S(nm, shp)

        guf = sb["gu"][:].bitcast(F32)
        A_tS = guf[:, 0:4096]
        A_dS = guf[:, 4096:8192]
        A_mag = guf[0:64, 8192:12288]
        A_nrt = guf[0:64, 12288:16384]
        arf = sb["arena"][:].bitcast(F32)
        A_kab = arf[0:8, 0:4096]
        fan_names = ["k00r", "k01r", "k10r", "k11r", "k00i", "k01i", "k10i", "k11i"]
        fanmap = {nm: arf[0:64, 4096 + i * 64: 4096 + (i + 1) * 64]
                  for i, nm in enumerate(fan_names)}
        mids = {nm: arf[0:64, 4608 + i * 64: 4608 + (i + 1) * 64]
                for i, nm in enumerate(["w_re", "w_imP", "n_re", "n_im", "s_re", "s_im",
                                        "tm1", "tm2", "tm3", "st_re", "st_im"])}
        A_aS = arf[:, 5312:5376]
        A_TS = arf[:, 5376:5440]
        A_vt = arf[0:64, 5440:5568]
        A_B = arf[0:64, 5568:5570]
        A_exp = arf[0:1, 5632:6400]
        wbufB = [sb["arena"][:, 16384 + i * 512:16384 + (i + 1) * 512] for i in range(4)]
        sb["Vc"] = arf[0:64, 7168:7296]
        sb["ident_f"] = arf[:, 7296:7424]
        sb["s2b"] = arf[:, 7424:7512]
        sb["Ksb"] = arf[0:64, 7512:7576]
        sb["exp_lo"] = arf[0:1, 6400:6476]
        sb["exp_h"] = arf[0:1, 6528:6604]
        sb["exp_e"] = arf[0:1, 6656:6732]
        sb["gsl"] = arf[0:86, 6784:6912]
        sb["usl"] = arf[0:86, 6912:7040]

        gsbf = sb["dring"][:].bitcast(F32)
        g_sb = [gsbf[:, j * 512:(j + 1) * 512] for j in range(4)]

        P = [es.enter_context(nc.psum_tensor(f"P{i}", [128, 512], F32)) for i in range(8)]
        PA, PB = P[0:4], P[4:8]

        p = Prog(nc, es)
        LD = p.sem("LD")
        KP = p.sem("KP")
        KZ = p.sem("KZ")
        XT = p.sem("XT")
        XP = [p.sem("XP0"), p.sem("XP1")]
        STS = [p.sem("ST0"), p.sem("ST1")]
        WIN = [p.sem(f"WIN{i}") for i in range(4)]
        XQ = [p.sem("XQ0"), p.sem("XQ1")]
        SH = p.sem("SH")
        WSL = [p.sem(f"W{i}") for i in range(NSLOT)]
        DSL = [p.sem("D0"), p.sem("D1")]
        p.sem("VECTORCH")
        p.sem("SCALARCH")
        TM = p.sem("TM")
        block = es.enter_context(nc.Block())

        psA = [None, None, None, None]

        def recip_nr(dst, src_ap, tmp_ap, extra_waits=()):
            p.op("vector", lambda h: h.reciprocal(dst, src_ap), waits=extra_waits)
            p.op("vector", lambda h: h.tensor_tensor(tmp_ap, src_ap, dst, ALU.mult))
            p.op("vector", lambda h: h.tensor_scalar(tmp_ap, tmp_ap, -1.0, 2.0, ALU.mult, ALU.add))
            return p.op("vector", lambda h: h.tensor_tensor(dst, dst, tmp_ap, ALU.mult))

        ln = {"xq_guard": [None, None], "xq_prev": [None, None], "xnb_guard": None}

        def emit_ln_stats(cnk, tiles):
            tok = None
            for tt in tiles:
                row0 = cnk * CH + tt * 128
                tbs = None
                for qd in range(4):
                    slq = qd % 2
                    xqs = sb["xq"][:, slq * 1024:(slq + 1) * 1024]
                    tql = p.op("sync", lambda h, row0=row0, qd=qd, xqs=xqs: h.dma_start(
                        out=xqs, in_=D["x"][row0:row0 + 128, qd * 1024:(qd + 1) * 1024]),
                        waits=[ln["xq_guard"][slq], ln["xq_prev"][slq]], sem=XQ[slq], dma=True)
                    ln["xq_prev"][slq] = tql
                    for half in range(2):
                        tbs = p.op("vector", lambda h, qd=qd, half=half, xqs=xqs: h.bn_stats(
                            sb["stats"][:, (qd * 2 + half) * 6:(qd * 2 + half + 1) * 6],
                            xqs[:, half * 512:(half + 1) * 512]),
                            waits=[tql] if half == 0 else ())
                    ln["xq_guard"][slq] = tbs
                p.op("vector", lambda h: h.bn_aggr(sb["mv"][:], sb["stats"][:]))
                rc = sb["rstdT"][:, tt:tt + 1]
                tve = p.op("vector", lambda h, rc=rc: h.tensor_scalar(rc, sb["mv"][:, 1:2],
                                                                      1.0, 1e-5, ALU.mult, ALU.add))
                p.op("scalar", lambda h, rc=rc: h.activation(rc, rc, AF.Ln), waits=[tve])
                tl2 = p.op("scalar", lambda h, rc=rc: h.activation(rc, rc, AF.Exp, scale=-0.5))
                tok = p.op("vector", lambda h, tt=tt, rc=rc: h.scalar_tensor_tensor(
                    sb["nmrT"][:, tt:tt + 1], sb["mv"][:, 0:1], -1.0, rc, ALU.mult, ALU.mult),
                    waits=[tl2])
                ln[("nmr", cnk, tt)] = tok
            return tok

        def emit_ln_apply(cnk, tiles):
            tev = None
            for tt in tiles:
                row0 = cnk * CH + tt * 128
                rc = sb["rstdT"][:, tt:tt + 1]
                nm = sb["nmrT"][:, tt:tt + 1]
                tacts = []
                for qd in range(4):
                    slq = qd % 2
                    xqs = sb["xq"][:, slq * 1024:(slq + 1) * 1024]
                    tql = p.op("sync", lambda h, row0=row0, qd=qd, xqs=xqs: h.dma_start(
                        out=xqs, in_=D["x"][row0:row0 + 128, qd * 1024:(qd + 1) * 1024]),
                        waits=[ln["xq_guard"][slq], ln["xq_prev"][slq]], sem=XQ[slq], dma=True)
                    ln["xq_prev"][slq] = tql
                    tact = p.op("scalar", lambda h, qd=qd, xqs=xqs, rc=rc, nm=nm: h.activation(
                        sb["xnb"][:, qd * 1024:(qd + 1) * 1024], xqs, AF.Identity,
                        bias=nm, scale=rc),
                        waits=[tql, ln["xnb_guard"], ln[("nmr", cnk, tt)]] if qd == 0 else [tql])
                    tacts.append(tact)
                    ln["xq_guard"][slq] = tact
                for i in range(32):
                    pj = i % 2
                    ttr = p.op("tensor", lambda h, i=i, pj=pj: h.transpose(
                        P[pj].ap().bitcast(BF16)[:, 0:128],
                        sb["xnb"][:, i * 128:(i + 1) * 128], sb["ident_bf"][:]),
                        waits=[tacts[i // 8], tid_bf, psA[pj]], sem=TM)
                    tev = p.op("scalar", lambda h, i=i, tt=tt, pj=pj: h.copy(
                        sb["xnT"][:, i * 512 + tt * 128: i * 512 + (tt + 1) * 128],
                        P[pj].ap().bitcast(BF16)[:, 0:128]), waits=[ttr])
                    psA[pj] = tev
                    if i == 31:
                        ln["xnb_guard"] = ttr
            return tev

        tload = {}
        for name in ["Lam", "pri", "qri", "Vc", "Ct", "Bp", "Dsc", "lstep",
                     "c_re", "c_im", "Es1re", "Es1im", "Es3", "tw_re", "tw_im",
                     "Rrev", "ident_f", "ones_row", "exp_lo", "exp_h", "exp_e",
                     "gs", "us", "ds"]:
            if name == "ds":
                continue
            dst = {"gs": "gsl", "us": "usl"}.get(name, name)
            if dst in ("exp_lo", "exp_h", "exp_e", "gsl", "usl", "Vc", "ident_f"):
                tload[name] = p.op("sync", lambda h, n=name, d=dst: h.dma_start(out=sb[d], in_=D[n]),
                                   sem=LD, dma=True)
            else:
                tload[name] = p.op("sync", lambda h, n=name, d=dst: h.dma_start(out=sb[d][:], in_=D[n]),
                                   sem=LD, dma=True)
        tload["tS"] = p.op("sync", lambda h: h.dma_start(out=A_tS, in_=D["tS"]), sem=LD, dma=True)
        ALL_LOADS = (LD, LD["n"])
        for k in list(tload):
            tload[k] = ALL_LOADS

        tid_bf = p.op("vector", lambda h: h.tensor_copy(sb["ident_bf"][:], sb["ident_f"]),
                      waits=[tload["ident_f"]])
        trv_bf = p.op("vector", lambda h: h.tensor_copy(sb["Rrev_bf"][:], sb["Rrev"][:]),
                      waits=[tload["Rrev"]])

        t1 = p.op("tensor", lambda h: h.transpose(P[0][:, 0:86], sb["gsl"], sb["ident_f"][0:86, 0:86]),
                  waits=[tload["gs"], tload["ident_f"]], sem=TM)
        t2 = p.op("tensor", lambda h: h.transpose(P[1][:, 0:86], sb["usl"], sb["ident_f"][0:86, 0:86]),
                  waits=[tload["us"]], sem=TM)
        t3 = p.op("scalar", lambda h: h.copy(sb["s2"][:, 0:86], P[0][:, 0:86]), waits=[t1])
        t4 = p.op("scalar", lambda h: h.copy(sb["s2b"][:, 0:86], P[1][:, 0:86]), waits=[t2])
        ts2 = p.op("vector", lambda h: h.tensor_tensor(sb["s2"][:, 0:86], sb["s2"][:, 0:86],
                                                       sb["s2b"][:, 0:86], ALU.mult), waits=[t3, t4])
        psA[0], psA[1] = t3, t4
        for _tt in range(4):
            emit_ln_stats(0, [_tt])
            t_ln_done0 = emit_ln_apply(0, [_tt])

        dsrow = gsbf[0:1, 0:2048]
        tsbc = {}
        tdsh = {}
        for half in range(2):
            wds = [tload["ones_row"]] if half == 0 else [tsbc[3]]
            tdsh[half] = p.op("gpsimd", lambda h, half=half: h.dma_start(
                out=dsrow, in_=D["ds"][0:1, half * 2048:(half + 1) * 2048]),
                waits=wds, sem=KP, dma=True)
            for cc in range(4):
                c = half * 4 + cc
                tm = p.op("tensor", lambda h, c=c, cc=cc: h.matmul(P[2 + (c % 2)][:], sb["ones_row"][:],
                          dsrow[0:1, cc * 512:(cc + 1) * 512], start=True, stop=True),
                          waits=[tdsh[half]] + ([tsbc[c - 2]] if c >= 2 else []), sem=TM)
                tsbc[c] = p.op("scalar", lambda h, c=c: h.copy(sb["s_bc"][:, c * 512:(c + 1) * 512],
                                                               P[2 + (c % 2)][:]), waits=[tm])
        t_sbc_done = tsbc[7]
        tds_kp = (KP, KP["n"])

        ex1, ex2, ex3 = A_exp[0:1, 0:76], A_exp[0:1, 128:204], A_exp[0:1, 256:332]
        s11, s12, s13 = A_exp[0:1, 400:401], A_exp[0:1, 432:433], A_exp[0:1, 464:465]
        sstep, snrt = A_exp[0:1, 496:497], A_exp[0:1, 528:529]
        xs = sb["lstep"][0:1, 0:1]
        wv0 = [tload["lstep"], tload["exp_lo"], tload["exp_h"], tload["exp_e"]]
        p.op("vector", lambda h: h.tensor_scalar(ex1, sb["exp_lo"], xs, None, ALU.is_le), waits=wv0)
        p.op("vector", lambda h: h.tensor_scalar(ex2, sb["exp_lo"], 0.0625, xs, ALU.add, ALU.is_gt))
        p.op("vector", lambda h: h.tensor_tensor(ex1, ex1, ex2, ALU.mult))
        p.op("vector", lambda h: h.tensor_tensor(ex2, ex1, sb["exp_h"], ALU.mult))
        p.op("vector", lambda h: h.tensor_reduce(s12, ex2, AX.X, ALU.add))
        p.op("vector", lambda h: h.tensor_tensor(ex3, ex1, sb["exp_e"], ALU.mult))
        p.op("vector", lambda h: h.tensor_reduce(s13, ex3, AX.X, ALU.add))
        p.op("vector", lambda h: h.tensor_tensor(s11, xs, s12, ALU.subtract))
        p.op("vector", lambda h: h.tensor_scalar(s12, s11, 1.0 / 3.0, 1.0, ALU.mult, ALU.add))
        p.op("vector", lambda h: h.tensor_tensor(s12, s11, s12, ALU.mult))
        p.op("vector", lambda h: h.tensor_scalar(s12, s12, 0.5, 1.0, ALU.mult, ALU.add))
        p.op("vector", lambda h: h.tensor_tensor(s12, s11, s12, ALU.mult))
        p.op("vector", lambda h: h.tensor_scalar(s12, s12, 1.0, 1.0, ALU.mult, ALU.add))
        p.op("vector", lambda h: h.tensor_tensor(sstep, s13, s12, ALU.mult))
        recip_nr(sb["step2"][:], sstep, snrt)
        t3v = p.op("vector", lambda h: h.tensor_scalar(sb["step2"][:], sb["step2"][:], 2.0, None, ALU.mult))
        t4m = p.op("tensor", lambda h: h.matmul(P[4][:, 0:1], sb["ones_row"][0:1, :], sb["step2"][:],
                                                start=True, stop=True), waits=[t3v, tload["ones_row"]], sem=TM)
        t_g2 = p.op("scalar", lambda h: h.copy(sb["g2x"][:], P[4][:, 0:1]), waits=[t4m])
        p.op("vector", lambda h: h.tensor_copy(sb["lamS"][0:64, :], sb["Lam"][:, 0:1]), waits=[tload["Lam"]])
        t_lamS = p.op("vector", lambda h: h.tensor_copy(sb["lamS"][64:128, :], sb["Lam"][:, 1:2]))

        A_vtre, A_vtim = A_vt[:, 0:64], A_vt[:, 64:128]
        t5 = p.op("tensor", lambda h: h.transpose(P[5][0:64, 0:64], sb["Vc"][:, 0:128:2], sb["ident_f"][0:64, 0:64]),
                  waits=[tload["Vc"], tload["ident_f"]], sem=TM)
        t6 = p.op("tensor", lambda h: h.transpose(P[6][0:64, 0:64], sb["Vc"][:, 1:128:2], sb["ident_f"][0:64, 0:64]),
                  sem=TM)
        t7 = p.op("scalar", lambda h: h.copy(A_vtre, P[5][0:64, 0:64]), waits=[t5])
        t8 = p.op("scalar", lambda h: h.copy(A_vtim, P[6][0:64, 0:64]), waits=[t6])
        t9 = p.op("tensor", lambda h: h.matmul(P[5][0:64, 0:1], A_vtre, sb["Bp"][:], start=True, stop=True),
                  waits=[t7, tload["Bp"]], sem=TM)
        t10 = p.op("tensor", lambda h: h.matmul(P[6][0:64, 0:1], A_vtim, sb["Bp"][:], start=True, stop=True),
                   waits=[t8], sem=TM)
        B_re, B_im = A_B[:, 0:1], A_B[:, 1:2]
        tBr = p.op("scalar", lambda h: h.copy(B_re, P[5][0:64, 0:1]), waits=[t9])
        tBi = p.op("scalar", lambda h: h.copy(B_im, P[6][0:64, 0:1]), waits=[t10])

        ct_r, ct_i = sb["Ct"][:, 0:1], sb["Ct"][:, 1:2]
        q_r, q_i = sb["qri"][:, 0:1], sb["qri"][:, 1:2]
        p_r, p_i = sb["pri"][:, 0:1], sb["pri"][:, 1:2]
        tm1c, tm2c = mids["tm1"][:, 0:1], mids["tm2"][:, 0:1]
        LT = sb["lhsTS"]
        tv = [tBr, tBi, tload["Ct"], tload["qri"], tload["pri"]]
        for ar, ai, br, bi, col in [(ct_r, ct_i, B_re, B_im, 0), (ct_r, ct_i, p_r, p_i, 1),
                                    (q_r, q_i, B_re, B_im, 2), (q_r, q_i, p_r, p_i, 3)]:
            p.op("vector", lambda h, a=ar, b=br: h.tensor_tensor(tm1c, a, b, ALU.mult), waits=tv)
            p.op("vector", lambda h, a=ai, b=bi: h.tensor_tensor(tm2c, a, b, ALU.mult))
            p.op("vector", lambda h, cx=col: h.tensor_tensor(LT[0:64, cx:cx + 1], tm1c, tm2c, ALU.add))
            p.op("vector", lambda h, a=ar, b=bi: h.tensor_tensor(tm1c, a, b, ALU.mult))
            p.op("vector", lambda h, a=ai, b=br: h.tensor_tensor(tm2c, a, b, ALU.mult))
            p.op("vector", lambda h, cx=col: h.tensor_tensor(LT[0:64, 4 + cx:5 + cx], tm1c, tm2c, ALU.subtract))
            p.op("vector", lambda h, cx=col: h.tensor_copy(LT[64:128, cx:cx + 1], LT[0:64, 4 + cx:5 + cx]))
            p.op("vector", lambda h, cx=col: h.tensor_scalar(LT[64:128, 4 + cx:5 + cx], LT[0:64, cx:cx + 1],
                                                             -1.0, None, ALU.mult))
        tlh = p.op("vector", lambda h: h.tensor_copy(tm1c, tm1c))

        tneg = p.op("vector", lambda h: h.tensor_scalar(sb["neglam"][:], sb["lamS"][:], -1.0, None, ALU.mult),
                    waits=[t_lamS])
        tds = p.op("scalar", lambda h: h.activation(A_dS, A_tS, AF.Identity,
                                                    bias=sb["neglam"][:], scale=sb["g2x"][:]),
                   waits=[t_g2, tneg, tload["tS"]])
        tsqA = p.op("scalar", lambda h: h.activation(A_tS[0:64, :], A_dS[0:64, :], AF.Square), waits=[tds])
        tsqB = p.op("vector", lambda h: h.tensor_tensor(A_mag, A_dS[64:128, :], A_dS[64:128, :], ALU.mult),
                    waits=[tds])
        tsum = p.op("vector", lambda h: h.tensor_tensor(A_tS[0:64, :], A_tS[0:64, :], A_mag, ALU.add),
                    waits=[tsqA])
        trc = p.op("vector", lambda h: h.reciprocal(A_mag, A_tS[0:64, :]))
        tcp = p.op("scalar", lambda h: h.copy(A_tS[64:128, :], A_mag), waits=[trc])
        p.op("vector", lambda h: h.tensor_tensor(A_dS[0:64, :], A_dS[0:64, :], A_mag, ALU.mult), waits=[trc])
        tinv = p.op("vector", lambda h: h.tensor_tensor(A_dS[64:128, :], A_dS[64:128, :],
                                                        A_tS[64:128, :], ALU.mult), waits=[tcp])

        tkabev = {}
        for kc in range(8):
            w = [tlh, tinv] + ([tkabev[kc - 2]] if kc >= 2 else [psA[0], psA[1]])
            tmm = p.op("tensor", lambda h, kc=kc: h.matmul(P[kc % 2][0:8, :], LT[:],
                       A_dS[:, kc * 512:(kc + 1) * 512], start=True, stop=True), waits=w, sem=TM)
            tkabev[kc] = p.op("scalar", lambda h, kc=kc: h.copy(A_kab[:, kc * 512:(kc + 1) * 512],
                                                                P[kc % 2][0:8, :]), waits=[tmm])
        tkh = p.op("gpsimd", lambda h: h.dma_start(out=kabh.rearrange("r p f -> r (p f)"), in_=A_kab),
                   waits=[tkabev[kc] for kc in range(8)], sem=KP, dma=True)
        for r, nm in enumerate(fan_names):
            p.op("gpsimd", lambda h, r=r, nm=nm: h.dma_start(out=fanmap[nm], in_=kabh[r]),
                 waits=[tkh], sem=KP, dma=True)
        tfan = [(KP, KP["n"])]

        k00r, k01r, k10r, k11r = fanmap["k00r"], fanmap["k01r"], fanmap["k10r"], fanmap["k11r"]
        k00i, k01i, k10i, k11i = fanmap["k00i"], fanmap["k01i"], fanmap["k10i"], fanmap["k11i"]
        w_re, w_imP = mids["w_re"], mids["w_imP"]
        n_re, n_im = mids["n_re"], mids["n_im"]
        s_re, s_im = mids["s_re"], mids["s_im"]
        tm1f, tm2f, tm3f = mids["tm1"], mids["tm2"], mids["tm3"]
        p.op("vector", lambda h: h.tensor_scalar(tm1f, k11r, 1.0, None, ALU.add),
             waits=tfan + [tload["c_re"], tload["c_im"]])
        p.op("vector", lambda h: h.tensor_tensor(tm2f, tm1f, tm1f, ALU.mult))
        p.op("vector", lambda h: h.tensor_tensor(tm3f, k11i, k11i, ALU.mult))
        p.op("vector", lambda h: h.tensor_tensor(tm2f, tm2f, tm3f, ALU.add))
        recip_nr(w_imP, tm2f, tm3f)
        p.op("vector", lambda h: h.tensor_copy(tm2f, w_imP))
        p.op("vector", lambda h: h.tensor_tensor(w_re, tm1f, tm2f, ALU.mult))
        p.op("vector", lambda h: h.tensor_tensor(w_imP, k11i, tm2f, ALU.mult))
        p.op("vector", lambda h: h.tensor_tensor(tm1f, k01r, k10r, ALU.mult))
        p.op("vector", lambda h: h.tensor_tensor(tm2f, k01i, k10i, ALU.mult))
        p.op("vector", lambda h: h.tensor_tensor(n_re, tm1f, tm2f, ALU.subtract))
        p.op("vector", lambda h: h.tensor_tensor(tm1f, k01r, k10i, ALU.mult))
        p.op("vector", lambda h: h.tensor_tensor(tm2f, k01i, k10r, ALU.mult))
        p.op("vector", lambda h: h.tensor_tensor(n_im, tm1f, tm2f, ALU.add))
        p.op("vector", lambda h: h.tensor_tensor(tm1f, n_re, w_re, ALU.mult))
        p.op("vector", lambda h: h.tensor_tensor(tm2f, n_im, w_imP, ALU.mult))
        p.op("vector", lambda h: h.tensor_tensor(tm3f, tm1f, tm2f, ALU.add))
        p.op("vector", lambda h: h.tensor_tensor(s_re, k00r, tm3f, ALU.subtract))
        p.op("vector", lambda h: h.tensor_tensor(tm1f, n_im, w_re, ALU.mult))
        p.op("vector", lambda h: h.tensor_tensor(tm2f, n_re, w_imP, ALU.mult))
        p.op("vector", lambda h: h.tensor_tensor(tm3f, tm1f, tm2f, ALU.subtract))
        p.op("vector", lambda h: h.tensor_tensor(s_im, k00i, tm3f, ALU.subtract))
        p.op("vector", lambda h: h.tensor_tensor(tm1f, sb["c_re"][:], s_re, ALU.mult))
        p.op("vector", lambda h: h.tensor_tensor(tm2f, sb["c_im"][:], s_im, ALU.mult))
        p.op("vector", lambda h: h.tensor_tensor(A_aS[0:64, :], tm1f, tm2f, ALU.subtract))
        p.op("vector", lambda h: h.tensor_tensor(tm1f, sb["c_re"][:], s_im, ALU.mult))
        p.op("vector", lambda h: h.tensor_tensor(tm2f, sb["c_im"][:], s_re, ALU.mult))
        tat = p.op("vector", lambda h: h.tensor_tensor(A_aS[64:128, :], tm1f, tm2f, ALU.add))

        ts1 = p.op("tensor", lambda h: h.matmul(P[2][0:64, 0:64], A_aS, sb["Es1re"][:], start=True, stop=True),
                   waits=[tat, tload["Es1re"], tload["Es1im"], t_sbc_done], sem=TM)
        ts2m = p.op("tensor", lambda h: h.matmul(P[3][0:64, 0:64], A_aS, sb["Es1im"][:], start=True, stop=True),
                    sem=TM)
        st_re, st_im = mids["st_re"], mids["st_im"]
        te1 = p.op("scalar", lambda h: h.copy(st_re, P[2][0:64, 0:64]), waits=[ts1])
        te2 = p.op("scalar", lambda h: h.copy(st_im, P[3][0:64, 0:64]), waits=[ts2m])
        p.op("vector", lambda h: h.tensor_tensor(tm1f, st_re, sb["tw_re"][:], ALU.mult),
             waits=[te1, te2, tload["tw_re"], tload["tw_im"]])
        p.op("vector", lambda h: h.tensor_tensor(tm2f, st_im, sb["tw_im"][:], ALU.mult))
        p.op("vector", lambda h: h.tensor_tensor(A_TS[0:64, :], tm1f, tm2f, ALU.subtract))
        p.op("vector", lambda h: h.tensor_tensor(tm1f, st_re, sb["tw_im"][:], ALU.mult))
        p.op("vector", lambda h: h.tensor_tensor(tm2f, st_im, sb["tw_re"][:], ALU.mult))
        ttw = p.op("vector", lambda h: h.tensor_tensor(A_TS[64:128, :], tm1f, tm2f, ALU.add))
        ts3 = p.op("tensor", lambda h: h.matmul(P[2][0:64, 0:64], sb["Es3"][:], A_TS, start=True, stop=True),
                   waits=[ttw, tload["Es3"], te1], sem=TM)
        tk1 = p.op("scalar", lambda h: h.copy(sb["Ksb"], P[2][0:64, 0:64]), waits=[ts3])
        tkD = p.op("scalar", lambda h: h.activation(sb["Ksb"][0:1, 0:1], sb["Dsc"][:], AF.Identity,
                                                    bias=sb["Ksb"][0:1, 0:1], scale=1.0),
                   waits=[tk1, tload["Dsc"]])
        tz = p.op("vector", lambda h: h.memset(sb["tmp"][0:1, 0:PAD], 0.0))
        tz1 = p.op("gpsimd", lambda h: h.dma_start(out=kpad[0:PAD].rearrange("(a b) -> a b", a=1),
                                                   in_=sb["tmp"][0:1, 0:PAD]), waits=[tz], sem=KZ, dma=True)
        tz2 = p.op("gpsimd", lambda h: h.dma_start(out=kpad[PAD:PAD + L].rearrange("(a b) -> a b", a=64),
                                                   in_=sb["Ksb"]), waits=[tkD], sem=KZ, dma=True)
        tkz_all = (KZ, KZ["n"])

        psA[0], psA[1], psA[2], psA[3] = tkabev[6], tkabev[7], tk1, te2
        psB = [t_g2, tBr, tBi, None]
        G46 = [tat, ttw, tinv, ts1, ts2m, ts3, t9, t10, t1, t2, ALL_LOADS]
        slot_guards = {0: [tkh], 1: [tkh], 2: [tkh], 3: [tkh],
                       4: G46, 5: G46, 6: G46,
                       7: [t5, t6, ts2, t4, tkz_all, ALL_LOADS], 8: []}
        slot_first = [True] * NSLOT
        t_sh_store = None

        slot_free = [None] * NSLOT
        slot_prev = [None] * NSLOT
        dslot_free = [None, None]
        dslot_prev = [None, None]
        win_prev = [None] * 4
        xp_prev = [None, None]
        xp_guard = [None, None]
        tmp_guard = [tkz_all, tkz_all]

        ssm_sb_free = [[None] * 4, [None] * 4]
        gsb_free = [t_sbc_done] * 4
        prev_chunk_ssm_done = None
        prev_strip_load = None
        prev_strip_ev = None
        gsb_prev = [None] * 4
        ln_apply_tok = {}
        last_gu_read = None
        t_arena_free = None
        slot_idx = 0
        t_out_all = []

        for cnk in range(NCH):
            t_ln_done = t_ln_done0 if cnk == 0 else ln_apply_tok[cnk]

            tok_gu_w = {}
            arena_guard = [prev_chunk_ssm_done, prev_strip_load]
            for gi, (ib0, gw) in enumerate(IGROUPS):
                for mat in ("g", "u"):
                    psum_set = PA if mat == "g" else PB
                    pf = psA if mat == "g" else psB
                    tok_piece = {}
                    slot_of = {}
                    for q in range(8):
                        sl = slot_idx % NSLOT
                        slot_idx += 1
                        wpd = [slot_free[sl], slot_prev[sl]] + arena_guard
                        if cnk == 0 and slot_first[sl]:
                            wpd += slot_guards[sl]
                            slot_first[sl] = False
                        src_ap = (D[mat + "main"][gi, q] if gw == 4 else D[mat + "tail"][q])
                        dst = sb["arena"][:, sl * 2048: sl * 2048 + 4 * gw * 128]
                        tok_piece[q] = p.op("gpsimd", lambda h, src_ap=src_ap, dst=dst: h.dma_start(
                            out=dst, in_=src_ap.rearrange("p hb i -> p (hb i)")),
                            waits=wpd, sem=WSL[sl], dma=True)
                        slot_prev[sl] = tok_piece[q]
                        slot_of[q] = sl
                    for j in range(gw):
                        tbank = None
                        for hb in range(32):
                            q = hb // 4
                            sl = slot_of[q]
                            give_tok = (hb == 31) or (j == gw - 1 and hb % 4 == 3)
                            wmm = []
                            if hb == 0:
                                wmm = [pf[j]]
                                if j == 0:
                                    wmm += [tok_piece[0], t_ln_done]
                            elif hb % 4 == 0 and j == 0:
                                wmm = [tok_piece[q]]
                            tmm = p.op("tensor", lambda h, sl=sl, hb=hb, j=j, gw=gw, pj=psum_set[j]: h.matmul(
                                pj[:],
                                sb["arena"][:, sl * 2048 + (hb % 4) * (gw * 128) + j * 128:
                                            sl * 2048 + (hb % 4) * (gw * 128) + (j + 1) * 128],
                                sb["xnT"][:, hb * 512:(hb + 1) * 512],
                                start=(hb == 0), stop=(hb == 31)),
                                waits=wmm, sem=TM if give_tok else None)
                            if give_tok:
                                if hb == 31:
                                    tbank = tmm
                                if j == gw - 1 and hb % 4 == 3:
                                    slot_free[sl] = tmm
                        if mat == "g":
                            tgc = p.op("scalar", lambda h, j=j: h.copy(g_sb[j], PA[j][:]),
                                       waits=[tbank, gsb_free[j], dslot_prev[j // 2], dslot_free[j // 2], tds_kp])
                            gsb_prev[j] = tgc
                            psA[j] = tgc
                            tok_gu_w[("gsb", gi, j)] = tgc
                        else:
                            ib = ib0 + j
                            wst = [tbank, tok_gu_w[("gsb", gi, j)], ts2]
                            if cnk == 0 and ib < 64:
                                wst.append(tkabev[7])
                            tst = p.op("vector", lambda h, ib=ib, j=j: h.scalar_tensor_tensor(
                                sb["gu"][:, ib * 512:(ib + 1) * 512], PB[j][:],
                                sb["s2"][:, ib:ib + 1], g_sb[j], ALU.mult, ALU.mult), waits=wst)
                            psB[j] = tst
                            gsb_free[j] = tst
                            tok_gu_w[ib] = tst
            t_gu_last = tok_gu_w[NIB - 1]
            if cnk + 1 < NCH:
                emit_ln_stats(cnk + 1, [0, 1, 2, 3])

            if cnk == 0:
                t_strip0 = {}
                win_prev0 = [None] * 4

                def win_dma0(h, off, wb):
                    w = kpad.copy()
                    w.offset = off
                    w.ap[0] = [1, 128]
                    w.ap.append([1, 512])
                    return h.dma_start(out=wb, in_=w)
                for s, dd in enumerate(range(-3, 29)):
                    off = PAD - 127 + 128 * dd
                    wb = wbufB[s % 4]
                    wwin = [tkz_all, t_gu_last, win_prev0[s % 4]] + ([t_strip0[s - 4]] if s >= 4 else [])
                    twn = p.op("gpsimd", lambda h, off=off, wb=wb: win_dma0(h, off, wb),
                               waits=wwin, sem=WIN[s % 4], dma=True)
                    win_prev0[s % 4] = twn
                    wrm = [twn, trv_bf, psA[s % 2], t_gu_last]
                    tsm = p.op("tensor", lambda h, s=s, wb=wb: h.matmul(P[s % 2][:], sb["Rrev_bf"][:], wb,
                               start=True, stop=True), waits=wrm, sem=TM)
                    t_strip0[s] = p.op("scalar", lambda h, s=s: h.copy(
                        sb["arena"][:, s * 512:(s + 1) * 512], P[s % 2][:]),
                        waits=[tsm, slot_prev[s // 4]])
                    psA[s % 2] = t_strip0[s]
                t_sh_store = p.op("gpsimd", lambda h: h.dma_start(out=striph, in_=sb["arena"][:, 0:16384]),
                                  waits=[t_strip0[s] for s in range(32)], sem=SH, dma=True)


                t_sload = t_strip0[31]
            else:
                t_sload = p.op("gpsimd", lambda h: h.dma_start(out=sb["arena"][:, 0:16384], in_=striph),
                               waits=[t_gu_last, prev_strip_load] + [slot_prev[i] for i in range(NSLOT)]
                                     + [slot_free[i] for i in range(NSLOT)],
                               sem=SH, dma=True)
            prev_strip_load = t_sh_store if cnk == 0 else t_sload
            for i in range(8):
                slot_prev[i] = prev_strip_load

            ssm_par = [sb["ssm0"], sb["ssm1"]]
            t_ssm_ev = {}
            last_ssm_mm = None
            last_down_mm = None
            for c in range(8):
                par = c % 2
                Ilist = list(range(0, min(32, 4 * c + 4)))
                for tb in range(4):
                    tmm = None
                    for ix, I in enumerate(Ilist):
                        sidx = 4 * c - I + 3
                        last = (ix == len(Ilist) - 1)
                        wss = []
                        if ix == 0:
                            wss = [t_sload, psA[tb]]
                        tmm = p.op("tensor", lambda h, I=I, tb=tb, sidx=sidx, first=(ix == 0), last=last:
                                   h.matmul(PA[tb][:],
                                            sb["xnT"][:, I * 512 + tb * 128: I * 512 + (tb + 1) * 128],
                                            sb["arena"][:, sidx * 512:(sidx + 1) * 512],
                                            start=first, stop=last),
                                   waits=wss, sem=TM if last else None)
                    last_ssm_mm = tmm
                    tse = p.op("scalar", lambda h, par=par, tb=tb: h.copy(
                        ssm_par[par][:, tb * 512:(tb + 1) * 512], PA[tb][:]),
                        waits=[tmm, ssm_sb_free[par][tb]])
                    t_ssm_ev[(c, tb)] = tse
                    psA[tb] = tse
                if c == 7 and cnk + 1 < NCH:
                    ln_apply_tok[cnk + 1] = emit_ln_apply(cnk + 1, [0, 1, 2, 3])
                txp_c = {}
                for tb in range(2):
                    row0 = cnk * CH + tb * 128
                    sl2 = tb % 2
                    txp_c[tb] = p.op("scalar", lambda h, row0=row0, c=c, sl2=sl2: h.dma_start(
                        out=sb["xp"][:, sl2 * 512:(sl2 + 1) * 512],
                        in_=D["x"][row0:row0 + 128, c * 512:(c + 1) * 512]),
                        waits=[xp_guard[sl2], xp_prev[sl2]], sem=XP[sl2], dma=True)
                    xp_prev[sl2] = txp_c[tb]
                for jj, (ib0, gw4) in enumerate(IGROUPS):
                    dsl_i = jj % 2
                    wdp = [dslot_free[dsl_i], dslot_prev[dsl_i]]
                    if c == 0 and jj < 2:
                        wdp += [gsb_free[j] for j in range(4)] + [gsb_prev[j] for j in range(4)]
                    src = (D["dmain"][c, jj] if gw4 == 4 else D["dtail"][c])
                    dstp = sb["dring"][:, dsl_i * 2048: dsl_i * 2048 + gw4 * 512]
                    tdp = p.op("sync", lambda h, src=src, dstp=dstp: h.dma_start(
                        out=dstp.bitcast(U16), in_=src.rearrange("p j hw -> p (j hw)")),
                        waits=wdp, sem=DSL[dsl_i], dma=True)
                    dslot_prev[dsl_i] = tdp
                    tgrp = None
                    for j in range(gw4):
                        ib = ib0 + j
                        for tb in range(4):
                            wmm = []
                            if j == 0 and tb == 0:
                                wmm.append(tdp)
                            if ib == 0:
                                wmm += [psB[tb], t_gu_last]
                            give = (j == gw4 - 1 and tb == 3)
                            tmm = p.op("tensor", lambda h, ib=ib, tb=tb, dsl_i=dsl_i, j=j: h.matmul(
                                PB[tb][:],
                                sb["gu"][:, ib * 512 + tb * 128: ib * 512 + (tb + 1) * 128],
                                sb["dring"][:, dsl_i * 2048 + j * 512: dsl_i * 2048 + (j + 1) * 512],
                                start=(ib == 0), stop=(ib == NIB - 1)),
                                waits=wmm, sem=TM if give else None)
                            if give:
                                tgrp = tmm
                    dslot_free[dsl_i] = tgrp
                    if ib0 + gw4 == NIB:
                        last_down_mm = tgrp
                for tb in range(4):
                    row0 = cnk * CH + tb * 128
                    sl2 = tb % 2
                    if tb >= 2:
                        txp_c[tb] = p.op("scalar", lambda h, row0=row0, c=c, sl2=sl2: h.dma_start(
                            out=sb["xp"][:, sl2 * 512:(sl2 + 1) * 512],
                            in_=D["x"][row0:row0 + 128, c * 512:(c + 1) * 512]),
                            waits=[xp_guard[sl2], xp_prev[sl2]], sem=XP[sl2], dma=True)
                        xp_prev[sl2] = txp_c[tb]
                    txp = txp_c[tb]
                    tv1 = p.op("vector", lambda h, tb=tb, c=c, sl2=sl2: h.tensor_tensor(
                        sb["tmp"][:, sl2 * 512:(sl2 + 1) * 512], PB[tb][:],
                        sb["s_bc"][:, c * 512:(c + 1) * 512], ALU.mult),
                        waits=[last_down_mm, t_sbc_done, tmp_guard[sl2]])
                    tv2 = p.op("vector", lambda h, tb=tb, par=par, sl2=sl2: h.tensor_tensor(
                        sb["tmp"][:, sl2 * 512:(sl2 + 1) * 512],
                        sb["tmp"][:, sl2 * 512:(sl2 + 1) * 512],
                        ssm_par[par][:, tb * 512:(tb + 1) * 512], ALU.add),
                        waits=[t_ssm_ev[(c, tb)]])
                    tv3 = p.op("vector", lambda h, tb=tb, sl2=sl2: h.tensor_tensor(
                        sb["tmp"][:, sl2 * 512:(sl2 + 1) * 512],
                        sb["tmp"][:, sl2 * 512:(sl2 + 1) * 512],
                        sb["xp"][:, sl2 * 512:(sl2 + 1) * 512], ALU.add),
                        waits=[txp])
                    psB[tb] = tv1
                    ssm_sb_free[par][tb] = tv2
                    xp_guard[sl2] = tv3
                    tst_o = p.op("sync", lambda h, row0=row0, c=c, sl2=sl2: h.dma_start(
                        out=OUT[row0:row0 + 128, c * 512:(c + 1) * 512],
                        in_=sb["tmp"][:, sl2 * 512:(sl2 + 1) * 512]),
                        waits=[tv3], sem=STS[sl2], dma=True)
                    tmp_guard[sl2] = tst_o
                    t_out_all.append(tst_o)
            prev_chunk_ssm_done = last_ssm_mm
            last_gu_read = last_down_mm
            t_arena_free = last_ssm_mm

        p.op("sync", lambda h: h.nop(), waits=[(STS[0], STS[0]["n"]), (STS[1], STS[1]["n"])])

        p.replay(block)
    return nc


def kernel(L0_attn_u, L0_attn_Lambda_ri, L0_attn_p_ri, L0_attn_q_ri, L0_attn_Vc_ri,
           L0_attn_Ct_ri, L0_attn_Bp, L0_attn_D, L0_attn_log_step,
           L0_ffn_gate_wq, L0_ffn_gate_scale, L0_ffn_up_wq, L0_ffn_up_scale,
           L0_ffn_down_wq, L0_ffn_down_scale):
    tbl = host_tables()
    wst = stage_weights(np.asarray(L0_ffn_gate_wq), np.asarray(L0_ffn_up_wq),
                        np.asarray(L0_ffn_down_wq))
    xfull = np.ascontiguousarray(np.asarray(L0_attn_u, dtype=np.float32).reshape(8192, L))

    common = dict(
        Lam=np.ascontiguousarray(np.asarray(L0_attn_Lambda_ri, np.float32)),
        pri=np.ascontiguousarray(np.asarray(L0_attn_p_ri, np.float32)),
        qri=np.ascontiguousarray(np.asarray(L0_attn_q_ri, np.float32)),
        Vc=np.ascontiguousarray(np.asarray(L0_attn_Vc_ri, np.float32).reshape(64, 128)),
        Ct=np.ascontiguousarray(np.asarray(L0_attn_Ct_ri, np.float32).reshape(64, 2)),
        Bp=np.ascontiguousarray(np.asarray(L0_attn_Bp, np.float32)),
        Dsc=np.ascontiguousarray(np.asarray(L0_attn_D, np.float32).reshape(1, 1)),
        lstep=np.ascontiguousarray(np.asarray(L0_attn_log_step, np.float32).reshape(1, 1)),
        gs=np.ascontiguousarray(np.asarray(L0_ffn_gate_scale, np.float32).reshape(86, 128)),
        us=np.ascontiguousarray(np.asarray(L0_ffn_up_scale, np.float32).reshape(86, 128)),
        ds=np.ascontiguousarray(np.asarray(L0_ffn_down_scale, np.float32).reshape(1, L)),
        gmain=wst["gmain"], gtail=wst["gtail"],
        umain=wst["umain"], utail=wst["utail"],
        dmain=wst["dmain"], dtail=wst["dtail"],
        **tbl,
    )
    in_maps = []
    for c in range(NCORE):
        m = dict(common)
        m["x"] = np.ascontiguousarray(xfull[c * TOKC:(c + 1) * TOKC])
        in_maps.append(m)

    nc = build_program()
    res = run_bass_kernel_spmd(nc, in_maps, core_ids=list(range(NCORE)))
    out = np.concatenate([np.asarray(r["o"], np.float32) for r in res.results], 0)
    return out.reshape(2, 4096, L)


# version 14
# speedup vs baseline: 1.0230x; 1.0230x over previous
import numpy as np
import ml_dtypes
import contextlib
import concourse.bass as bass
import concourse.mybir as mybir
from concourse.bass_utils import run_bass_kernel_spmd
from concourse.alu_op_type import AluOpType as ALU
import concourse.bass_utils as _bu

if not getattr(_bu, "_ldw_opt_patched", False):
    _orig_run_command = _bu.run_command

    def _run_command_ldw(cmd, *a, **kw):
        cmd = [("--enable-ldw-opt=true" if c == "--enable-ldw-opt=false" else c) for c in cmd]
        return _orig_run_command(cmd, *a, **kw)

    _bu.run_command = _run_command_ldw
    _bu._ldw_opt_patched = True

F32 = mybir.dt.float32
BF16 = mybir.dt.bfloat16
I8 = mybir.dt.int8
U16 = mybir.dt.uint16
AF = mybir.ActivationFunctionType
AX = mybir.AxisListType

L = 4096
NCORE = 8
TOKC = 1024
CH = 512
NCH = TOKC // CH
NIB = 86
IGROUPS = [(g * 4, 4) for g in range(21)] + [(84, 2)]
PAD = 512
NSLOT = 9
ENGINES = ("sync", "gpsimd", "vector", "scalar", "tensor")


class Prog:

    def __init__(self, nc, es):
        self.nc = nc
        self.stream = {e: [] for e in ENGINES}
        self.sems = {}
        self.es = es

    def sem(self, name):
        h = self.es.enter_context(self.nc.semaphore(name))
        s = {"h": h, "n": 0, "name": name}
        self.sems[name] = s
        return s

    def op(self, eng, fn, waits=(), sem=None, dma=False):
        waits = [w for w in waits if w is not None]
        if eng in ("vector", "scalar") and not dma:
            chain = self.sems[eng.upper() + "CH"]
            if chain["n"] > 0:
                waits.append((chain, chain["n"]))
            assert sem is None or sem is chain
            sem = chain
            sem["n"] += 1
            self.stream[eng].append((fn, waits, sem, 1))
            return (sem, sem["n"])
        tok = None
        if sem is not None:
            sem["n"] += 16 if dma else 1
            tok = (sem, sem["n"])
        self.stream[eng].append((fn, waits, sem, 16 if dma else 1))
        return tok

    def replay(self, block):
        for eng in ENGINES:
            stream = self.stream[eng]
            if not stream:
                continue

            def make(stream):
                def body(h):
                    hw = {}
                    for fn, waits, sem, inc in stream:
                        for s, v in waits:
                            if hw.get(id(s), -1) < v:
                                h.wait_ge(s["h"], v)
                                hw[id(s)] = v
                        ins = fn(h)
                        if sem is not None:
                            ins.then_inc(sem["h"], inc)
                return body
            getattr(block, eng)(make(stream))


def host_tables():
    f32, f64 = np.float32, np.float64
    k = np.arange(L)
    x = (k / L).astype(f32)
    th = (x * np.float32(-2 * np.pi)).astype(f32)
    O = np.cos(th.astype(f64)).astype(f32).astype(f64) \
        + 1j * np.sin(th.astype(f64)).astype(f32).astype(f64)
    c = 2.0 / (1.0 + O)
    t = (1.0 - O) / (1.0 + O)
    n = np.arange(64)
    E = np.exp(2j * np.pi * np.outer(n, n) / 64.0)
    tw = np.exp(2j * np.pi * np.outer(n, n) / float(L))
    R = np.zeros((128, 128), f32)
    R[np.arange(128), 127 - np.arange(128)] = 1.0
    return {
        "c_re": np.ascontiguousarray(c.real.astype(f32).reshape(64, 64)),
        "c_im": np.ascontiguousarray(c.imag.astype(f32).reshape(64, 64)),
        "tS": np.ascontiguousarray(np.concatenate(
            [np.broadcast_to(t.real.astype(f32), (64, L)),
             np.broadcast_to(t.imag.astype(f32), (64, L))], 0)),
        "Es1re": np.ascontiguousarray(np.concatenate([E.real, -E.imag], 0).astype(f32)),
        "Es1im": np.ascontiguousarray(np.concatenate([E.imag, E.real], 0).astype(f32)),
        "Es3": np.ascontiguousarray(np.concatenate([E.real / L, -E.imag / L], 0).astype(f32)),
        "tw_re": np.ascontiguousarray(tw.real.astype(f32)),
        "tw_im": np.ascontiguousarray(tw.imag.astype(f32)),
        "Rrev": R,
        "ident_f": np.eye(128, dtype=f32),
        "ones_row": np.ones((1, 128), f32),
        "exp_lo": (np.arange(76) / 16.0 - 6.9375 - 1.0 / 32).astype(f32).reshape(1, 76),
        "exp_h": (np.arange(76) / 16.0 - 6.9375).astype(f32).reshape(1, 76),
        "exp_e": np.exp(np.arange(76) / 16.0 - 6.9375).astype(f32).reshape(1, 76),
    }


def stage_weights(gate_wq, up_wq, down_wq):
    def i8(w):
        return np.ascontiguousarray(w).view(np.int8)[:, ::4]

    out = {}
    for name, wq in (("g", gate_wq), ("u", up_wq)):
        wT = i8(wq).T if name == "u" else i8(wq).T.astype(ml_dtypes.bfloat16).view(np.uint16)
        out[name + "main"] = np.ascontiguousarray(
            wT[:, :10752].reshape(8, 4, 128, 21, 512).transpose(3, 0, 2, 1, 4))
        out[name + "tail"] = np.ascontiguousarray(
            wT[:, 10752:].reshape(8, 4, 128, 256).transpose(0, 2, 1, 3))
    dT = i8(down_wq).T.astype(ml_dtypes.bfloat16).view(np.uint16)
    out["dmain"] = np.ascontiguousarray(
        dT[:10752, :].reshape(21, 4, 128, 8, 512).transpose(3, 0, 2, 1, 4))
    out["dtail"] = np.ascontiguousarray(
        dT[10752:, :].reshape(2, 128, 8, 512).transpose(2, 1, 0, 3))
    return out


def build_program():
    nc = bass.Bass("TRN2", target_bir_lowering=False)
    D = {}
    dram_spec = [
        ("x", [TOKC, L], F32),
        ("Lam", [64, 2], F32), ("pri", [64, 2], F32), ("qri", [64, 2], F32),
        ("Vc", [64, 128], F32), ("Ct", [64, 2], F32), ("Bp", [64, 1], F32),
        ("Dsc", [1, 1], F32), ("lstep", [1, 1], F32),
        ("gs", [86, 128], F32), ("us", [86, 128], F32), ("ds", [1, L], F32),
        ("gmain", [21, 8, 128, 4, 512], U16), ("gtail", [8, 128, 4, 256], U16),
        ("umain", [21, 8, 128, 4, 512], I8), ("utail", [8, 128, 4, 256], I8),
        ("dmain", [8, 21, 128, 4, 512], U16), ("dtail", [8, 128, 2, 512], U16),
        ("c_re", [64, 64], F32), ("c_im", [64, 64], F32), ("tS", [128, L], F32),
        ("Es1re", [128, 64], F32), ("Es1im", [128, 64], F32), ("Es3", [128, 64], F32),
        ("tw_re", [64, 64], F32), ("tw_im", [64, 64], F32),
        ("Rrev", [128, 128], F32), ("ident_f", [128, 128], F32),
        ("ones_row", [1, 128], F32),
        ("exp_lo", [1, 76], F32), ("exp_h", [1, 76], F32), ("exp_e", [1, 76], F32),
    ]
    for name, shape, dt in dram_spec:
        D[name] = nc.dram_tensor(name, shape, dt, kind="ExternalInput").ap()
    OUT = nc.dram_tensor("o", [TOKC, L], F32, kind="ExternalOutput").ap()
    kpad = nc.dram_tensor("kpad", [PAD + L], F32, kind="Internal").ap()
    striph = nc.dram_tensor("striph", [128, 32 * 512], BF16, kind="Internal").ap()
    kabh = nc.dram_tensor("kabh", [8, 64, 64], F32, kind="Internal").ap()

    es = contextlib.ExitStack()
    with es:
        sb = {}

        def S(name, shape, dt=F32):
            sb[name] = es.enter_context(nc.sbuf_tensor("sb_" + name, shape, dt))
            return sb[name]

        S("xnT", [128, 32 * CH], BF16)
        S("gu", [128, NIB * CH], BF16)
        S("arena", [128, NSLOT * 2048], BF16)
        S("dring", [128, 2 * 2048], BF16)
        S("ssm0", [128, 4 * CH], BF16)
        S("ssm1", [128, 4 * CH], BF16)
        S("s_bc", [128, L], BF16)
        S("tmp", [128, 2 * CH], F32)
        S("xp", [128, 2 * CH], F32)
        S("xq", [128, 2048], F32)
        S("xnb", [128, L], BF16)
        S("ident_bf", [128, 128], BF16)
        S("Rrev_bf", [128, 128], BF16)
        S("Rrev", [128, 128], F32)
        S("ones_row", [1, 128], F32)
        S("s2", [128, 88], F32)
        S("stats", [128, 48], F32)
        S("mv", [128, 2], F32)
        S("rstdT", [128, 4], F32)
        S("nmrT", [128, 4], F32)
        for nm, shp in [("Lam", [64, 2]), ("pri", [64, 2]), ("qri", [64, 2]),
                        ("Ct", [64, 2]), ("Bp", [64, 1]),
                        ("Dsc", [1, 1]), ("lstep", [1, 1]),
                        ("c_re", [64, 64]), ("c_im", [64, 64]),
                        ("Es1re", [128, 64]), ("Es1im", [128, 64]), ("Es3", [128, 64]),
                        ("tw_re", [64, 64]), ("tw_im", [64, 64]),
                        ("step2", [1, 1]), ("g2x", [128, 1]), ("lamS", [128, 1]), ("neglam", [128, 1]),
                        ("lhsTS", [128, 8]),
                        ]:
            S(nm, shp)

        guf = sb["gu"][:].bitcast(F32)
        A_tS = guf[:, 0:4096]
        A_dS = guf[:, 4096:8192]
        A_mag = guf[0:64, 8192:12288]
        A_nrt = guf[0:64, 12288:16384]
        arf = sb["arena"][:].bitcast(F32)
        A_kab = arf[0:8, 0:4096]
        fan_names = ["k00r", "k01r", "k10r", "k11r", "k00i", "k01i", "k10i", "k11i"]
        fanmap = {nm: arf[0:64, 4096 + i * 64: 4096 + (i + 1) * 64]
                  for i, nm in enumerate(fan_names)}
        mids = {nm: arf[0:64, 4608 + i * 64: 4608 + (i + 1) * 64]
                for i, nm in enumerate(["w_re", "w_imP", "n_re", "n_im", "s_re", "s_im",
                                        "tm1", "tm2", "tm3", "st_re", "st_im"])}
        A_aS = arf[:, 5312:5376]
        A_TS = arf[:, 5376:5440]
        A_vt = arf[0:64, 5440:5568]
        A_B = arf[0:64, 5568:5570]
        A_exp = arf[0:1, 5632:6400]
        wbufB = [sb["arena"][:, 16384 + i * 512:16384 + (i + 1) * 512] for i in range(4)]
        sb["Vc"] = arf[0:64, 7168:7296]
        sb["ident_f"] = arf[:, 7296:7424]
        sb["s2b"] = arf[:, 7424:7512]
        sb["Ksb"] = arf[0:64, 7512:7576]
        sb["exp_lo"] = arf[0:1, 6400:6476]
        sb["exp_h"] = arf[0:1, 6528:6604]
        sb["exp_e"] = arf[0:1, 6656:6732]
        sb["gsl"] = arf[0:86, 6784:6912]
        sb["usl"] = arf[0:86, 6912:7040]

        gsbf = sb["dring"][:].bitcast(F32)
        g_sb = [gsbf[:, j * 512:(j + 1) * 512] for j in range(4)]

        P = [es.enter_context(nc.psum_tensor(f"P{i}", [128, 512], F32)) for i in range(8)]
        PA, PB = P[0:4], P[4:8]

        p = Prog(nc, es)
        LD = p.sem("LD")
        KP = p.sem("KP")
        KZ = p.sem("KZ")
        XT = p.sem("XT")
        XP = [p.sem("XP0"), p.sem("XP1")]
        STS = [p.sem("ST0"), p.sem("ST1")]
        WIN = [p.sem(f"WIN{i}") for i in range(4)]
        XQ = [p.sem("XQ0"), p.sem("XQ1")]
        SH = p.sem("SH")
        WSL = [p.sem(f"W{i}") for i in range(NSLOT)]
        DSL = [p.sem("D0"), p.sem("D1")]
        p.sem("VECTORCH")
        p.sem("SCALARCH")
        TM = p.sem("TM")
        block = es.enter_context(nc.Block())

        psA = [None, None, None, None]

        def recip_nr(dst, src_ap, tmp_ap, extra_waits=()):
            p.op("vector", lambda h: h.reciprocal(dst, src_ap), waits=extra_waits)
            p.op("vector", lambda h: h.tensor_tensor(tmp_ap, src_ap, dst, ALU.mult))
            p.op("vector", lambda h: h.tensor_scalar(tmp_ap, tmp_ap, -1.0, 2.0, ALU.mult, ALU.add))
            return p.op("vector", lambda h: h.tensor_tensor(dst, dst, tmp_ap, ALU.mult))

        ln = {"xq_guard": [None, None], "xq_prev": [None, None], "xnb_guard": None}

        def emit_ln_stats(cnk, tiles):
            tok = None
            for tt in tiles:
                row0 = cnk * CH + tt * 128
                tbs = None
                for qd in range(4):
                    slq = qd % 2
                    xqs = sb["xq"][:, slq * 1024:(slq + 1) * 1024]
                    tql = p.op("sync", lambda h, row0=row0, qd=qd, xqs=xqs: h.dma_start(
                        out=xqs, in_=D["x"][row0:row0 + 128, qd * 1024:(qd + 1) * 1024]),
                        waits=[ln["xq_guard"][slq], ln["xq_prev"][slq]], sem=XQ[slq], dma=True)
                    ln["xq_prev"][slq] = tql
                    for half in range(2):
                        tbs = p.op("vector", lambda h, qd=qd, half=half, xqs=xqs: h.bn_stats(
                            sb["stats"][:, (qd * 2 + half) * 6:(qd * 2 + half + 1) * 6],
                            xqs[:, half * 512:(half + 1) * 512]),
                            waits=[tql] if half == 0 else ())
                    ln["xq_guard"][slq] = tbs
                p.op("vector", lambda h: h.bn_aggr(sb["mv"][:], sb["stats"][:]))
                rc = sb["rstdT"][:, tt:tt + 1]
                tve = p.op("vector", lambda h, rc=rc: h.tensor_scalar(rc, sb["mv"][:, 1:2],
                                                                      1.0, 1e-5, ALU.mult, ALU.add))
                p.op("scalar", lambda h, rc=rc: h.activation(rc, rc, AF.Ln), waits=[tve])
                tl2 = p.op("scalar", lambda h, rc=rc: h.activation(rc, rc, AF.Exp, scale=-0.5))
                tok = p.op("vector", lambda h, tt=tt, rc=rc: h.scalar_tensor_tensor(
                    sb["nmrT"][:, tt:tt + 1], sb["mv"][:, 0:1], -1.0, rc, ALU.mult, ALU.mult),
                    waits=[tl2])
                ln[("nmr", cnk, tt)] = tok
            return tok

        def emit_ln_apply(cnk, tiles):
            tev = None
            for tt in tiles:
                row0 = cnk * CH + tt * 128
                rc = sb["rstdT"][:, tt:tt + 1]
                nm = sb["nmrT"][:, tt:tt + 1]
                tacts = []
                for qd in range(4):
                    slq = qd % 2
                    xqs = sb["xq"][:, slq * 1024:(slq + 1) * 1024]
                    tql = p.op("sync", lambda h, row0=row0, qd=qd, xqs=xqs: h.dma_start(
                        out=xqs, in_=D["x"][row0:row0 + 128, qd * 1024:(qd + 1) * 1024]),
                        waits=[ln["xq_guard"][slq], ln["xq_prev"][slq]], sem=XQ[slq], dma=True)
                    ln["xq_prev"][slq] = tql
                    tact = p.op("scalar", lambda h, qd=qd, xqs=xqs, rc=rc, nm=nm: h.activation(
                        sb["xnb"][:, qd * 1024:(qd + 1) * 1024], xqs, AF.Identity,
                        bias=nm, scale=rc),
                        waits=[tql, ln["xnb_guard"], ln[("nmr", cnk, tt)]] if qd == 0 else [tql])
                    tacts.append(tact)
                    ln["xq_guard"][slq] = tact
                for i in range(32):
                    pj = i % 2
                    ttr = p.op("tensor", lambda h, i=i, pj=pj: h.transpose(
                        P[pj].ap().bitcast(BF16)[:, 0:128],
                        sb["xnb"][:, i * 128:(i + 1) * 128], sb["ident_bf"][:]),
                        waits=[tacts[i // 8], tid_bf, psA[pj]], sem=TM)
                    tev = p.op("scalar", lambda h, i=i, tt=tt, pj=pj: h.copy(
                        sb["xnT"][:, i * 512 + tt * 128: i * 512 + (tt + 1) * 128],
                        P[pj].ap().bitcast(BF16)[:, 0:128]), waits=[ttr])
                    psA[pj] = tev
                    if i == 31:
                        ln["xnb_guard"] = ttr
            return tev

        tload = {}
        for name in ["Lam", "pri", "qri", "Vc", "Ct", "Bp", "Dsc", "lstep",
                     "c_re", "c_im", "Es1re", "Es1im", "Es3", "tw_re", "tw_im",
                     "Rrev", "ident_f", "ones_row", "exp_lo", "exp_h", "exp_e",
                     "gs", "us", "ds"]:
            if name == "ds":
                continue
            dst = {"gs": "gsl", "us": "usl"}.get(name, name)
            if dst in ("exp_lo", "exp_h", "exp_e", "gsl", "usl", "Vc", "ident_f"):
                tload[name] = p.op("sync", lambda h, n=name, d=dst: h.dma_start(out=sb[d], in_=D[n]),
                                   sem=LD, dma=True)
            else:
                tload[name] = p.op("sync", lambda h, n=name, d=dst: h.dma_start(out=sb[d][:], in_=D[n]),
                                   sem=LD, dma=True)
        tload["tS"] = p.op("sync", lambda h: h.dma_start(out=A_tS, in_=D["tS"]), sem=LD, dma=True)
        ALL_LOADS = (LD, LD["n"])
        for k in list(tload):
            tload[k] = ALL_LOADS

        tid_bf = p.op("vector", lambda h: h.tensor_copy(sb["ident_bf"][:], sb["ident_f"]),
                      waits=[tload["ident_f"]])
        trv_bf = p.op("vector", lambda h: h.tensor_copy(sb["Rrev_bf"][:], sb["Rrev"][:]),
                      waits=[tload["Rrev"]])

        t1 = p.op("tensor", lambda h: h.transpose(P[0][:, 0:86], sb["gsl"], sb["ident_f"][0:86, 0:86]),
                  waits=[tload["gs"], tload["ident_f"]], sem=TM)
        t2 = p.op("tensor", lambda h: h.transpose(P[1][:, 0:86], sb["usl"], sb["ident_f"][0:86, 0:86]),
                  waits=[tload["us"]], sem=TM)
        t3 = p.op("scalar", lambda h: h.copy(sb["s2"][:, 0:86], P[0][:, 0:86]), waits=[t1])
        t4 = p.op("scalar", lambda h: h.copy(sb["s2b"][:, 0:86], P[1][:, 0:86]), waits=[t2])
        ts2 = p.op("vector", lambda h: h.tensor_tensor(sb["s2"][:, 0:86], sb["s2"][:, 0:86],
                                                       sb["s2b"][:, 0:86], ALU.mult), waits=[t3, t4])
        psA[0], psA[1] = t3, t4
        for _tt in range(4):
            emit_ln_stats(0, [_tt])
            t_ln_done0 = emit_ln_apply(0, [_tt])

        dsrow = gsbf[0:1, 0:2048]
        tsbc = {}
        tdsh = {}
        for half in range(2):
            wds = [tload["ones_row"]] if half == 0 else [tsbc[3]]
            tdsh[half] = p.op("gpsimd", lambda h, half=half: h.dma_start(
                out=dsrow, in_=D["ds"][0:1, half * 2048:(half + 1) * 2048]),
                waits=wds, sem=KP, dma=True)
            for cc in range(4):
                c = half * 4 + cc
                tm = p.op("tensor", lambda h, c=c, cc=cc: h.matmul(P[2 + (c % 2)][:], sb["ones_row"][:],
                          dsrow[0:1, cc * 512:(cc + 1) * 512], start=True, stop=True),
                          waits=[tdsh[half]] + ([tsbc[c - 2]] if c >= 2 else []), sem=TM)
                tsbc[c] = p.op("scalar", lambda h, c=c: h.copy(sb["s_bc"][:, c * 512:(c + 1) * 512],
                                                               P[2 + (c % 2)][:]), waits=[tm])
        t_sbc_done = tsbc[7]
        tds_kp = (KP, KP["n"])

        ex1, ex2, ex3 = A_exp[0:1, 0:76], A_exp[0:1, 128:204], A_exp[0:1, 256:332]
        s11, s12, s13 = A_exp[0:1, 400:401], A_exp[0:1, 432:433], A_exp[0:1, 464:465]
        sstep, snrt = A_exp[0:1, 496:497], A_exp[0:1, 528:529]
        xs = sb["lstep"][0:1, 0:1]
        wv0 = [tload["lstep"], tload["exp_lo"], tload["exp_h"], tload["exp_e"]]
        p.op("vector", lambda h: h.tensor_scalar(ex1, sb["exp_lo"], xs, None, ALU.is_le), waits=wv0)
        p.op("vector", lambda h: h.tensor_scalar(ex2, sb["exp_lo"], 0.0625, xs, ALU.add, ALU.is_gt))
        p.op("vector", lambda h: h.tensor_tensor(ex1, ex1, ex2, ALU.mult))
        p.op("vector", lambda h: h.tensor_tensor(ex2, ex1, sb["exp_h"], ALU.mult))
        p.op("vector", lambda h: h.tensor_reduce(s12, ex2, AX.X, ALU.add))
        p.op("vector", lambda h: h.tensor_tensor(ex3, ex1, sb["exp_e"], ALU.mult))
        p.op("vector", lambda h: h.tensor_reduce(s13, ex3, AX.X, ALU.add))
        p.op("vector", lambda h: h.tensor_tensor(s11, xs, s12, ALU.subtract))
        p.op("vector", lambda h: h.tensor_scalar(s12, s11, 1.0 / 3.0, 1.0, ALU.mult, ALU.add))
        p.op("vector", lambda h: h.tensor_tensor(s12, s11, s12, ALU.mult))
        p.op("vector", lambda h: h.tensor_scalar(s12, s12, 0.5, 1.0, ALU.mult, ALU.add))
        p.op("vector", lambda h: h.tensor_tensor(s12, s11, s12, ALU.mult))
        p.op("vector", lambda h: h.tensor_scalar(s12, s12, 1.0, 1.0, ALU.mult, ALU.add))
        p.op("vector", lambda h: h.tensor_tensor(sstep, s13, s12, ALU.mult))
        recip_nr(sb["step2"][:], sstep, snrt)
        t3v = p.op("vector", lambda h: h.tensor_scalar(sb["step2"][:], sb["step2"][:], 2.0, None, ALU.mult))
        t4m = p.op("tensor", lambda h: h.matmul(P[4][:, 0:1], sb["ones_row"][0:1, :], sb["step2"][:],
                                                start=True, stop=True), waits=[t3v, tload["ones_row"]], sem=TM)
        t_g2 = p.op("scalar", lambda h: h.copy(sb["g2x"][:], P[4][:, 0:1]), waits=[t4m])
        p.op("vector", lambda h: h.tensor_copy(sb["lamS"][0:64, :], sb["Lam"][:, 0:1]), waits=[tload["Lam"]])
        t_lamS = p.op("vector", lambda h: h.tensor_copy(sb["lamS"][64:128, :], sb["Lam"][:, 1:2]))

        A_vtre, A_vtim = A_vt[:, 0:64], A_vt[:, 64:128]
        t5 = p.op("tensor", lambda h: h.transpose(P[5][0:64, 0:64], sb["Vc"][:, 0:128:2], sb["ident_f"][0:64, 0:64]),
                  waits=[tload["Vc"], tload["ident_f"]], sem=TM)
        t6 = p.op("tensor", lambda h: h.transpose(P[6][0:64, 0:64], sb["Vc"][:, 1:128:2], sb["ident_f"][0:64, 0:64]),
                  sem=TM)
        t7 = p.op("scalar", lambda h: h.copy(A_vtre, P[5][0:64, 0:64]), waits=[t5])
        t8 = p.op("scalar", lambda h: h.copy(A_vtim, P[6][0:64, 0:64]), waits=[t6])
        t9 = p.op("tensor", lambda h: h.matmul(P[5][0:64, 0:1], A_vtre, sb["Bp"][:], start=True, stop=True),
                  waits=[t7, tload["Bp"]], sem=TM)
        t10 = p.op("tensor", lambda h: h.matmul(P[6][0:64, 0:1], A_vtim, sb["Bp"][:], start=True, stop=True),
                   waits=[t8], sem=TM)
        B_re, B_im = A_B[:, 0:1], A_B[:, 1:2]
        tBr = p.op("scalar", lambda h: h.copy(B_re, P[5][0:64, 0:1]), waits=[t9])
        tBi = p.op("scalar", lambda h: h.copy(B_im, P[6][0:64, 0:1]), waits=[t10])

        ct_r, ct_i = sb["Ct"][:, 0:1], sb["Ct"][:, 1:2]
        q_r, q_i = sb["qri"][:, 0:1], sb["qri"][:, 1:2]
        p_r, p_i = sb["pri"][:, 0:1], sb["pri"][:, 1:2]
        tm1c, tm2c = mids["tm1"][:, 0:1], mids["tm2"][:, 0:1]
        LT = sb["lhsTS"]
        tv = [tBr, tBi, tload["Ct"], tload["qri"], tload["pri"]]
        for ar, ai, br, bi, col in [(ct_r, ct_i, B_re, B_im, 0), (ct_r, ct_i, p_r, p_i, 1),
                                    (q_r, q_i, B_re, B_im, 2), (q_r, q_i, p_r, p_i, 3)]:
            p.op("vector", lambda h, a=ar, b=br: h.tensor_tensor(tm1c, a, b, ALU.mult), waits=tv)
            p.op("vector", lambda h, a=ai, b=bi: h.tensor_tensor(tm2c, a, b, ALU.mult))
            p.op("vector", lambda h, cx=col: h.tensor_tensor(LT[0:64, cx:cx + 1], tm1c, tm2c, ALU.add))
            p.op("vector", lambda h, a=ar, b=bi: h.tensor_tensor(tm1c, a, b, ALU.mult))
            p.op("vector", lambda h, a=ai, b=br: h.tensor_tensor(tm2c, a, b, ALU.mult))
            p.op("vector", lambda h, cx=col: h.tensor_tensor(LT[0:64, 4 + cx:5 + cx], tm1c, tm2c, ALU.subtract))
            p.op("vector", lambda h, cx=col: h.tensor_copy(LT[64:128, cx:cx + 1], LT[0:64, 4 + cx:5 + cx]))
            p.op("vector", lambda h, cx=col: h.tensor_scalar(LT[64:128, 4 + cx:5 + cx], LT[0:64, cx:cx + 1],
                                                             -1.0, None, ALU.mult))
        tlh = p.op("vector", lambda h: h.tensor_copy(tm1c, tm1c))

        tneg = p.op("vector", lambda h: h.tensor_scalar(sb["neglam"][:], sb["lamS"][:], -1.0, None, ALU.mult),
                    waits=[t_lamS])
        tds = p.op("scalar", lambda h: h.activation(A_dS, A_tS, AF.Identity,
                                                    bias=sb["neglam"][:], scale=sb["g2x"][:]),
                   waits=[t_g2, tneg, tload["tS"]])
        tsqA = p.op("scalar", lambda h: h.activation(A_tS[0:64, :], A_dS[0:64, :], AF.Square), waits=[tds])
        tsqB = p.op("vector", lambda h: h.tensor_tensor(A_mag, A_dS[64:128, :], A_dS[64:128, :], ALU.mult),
                    waits=[tds])
        tsum = p.op("vector", lambda h: h.tensor_tensor(A_tS[0:64, :], A_tS[0:64, :], A_mag, ALU.add),
                    waits=[tsqA])
        trc = p.op("vector", lambda h: h.reciprocal(A_mag, A_tS[0:64, :]))
        tcp = p.op("scalar", lambda h: h.copy(A_tS[64:128, :], A_mag), waits=[trc])
        p.op("vector", lambda h: h.tensor_tensor(A_dS[0:64, :], A_dS[0:64, :], A_mag, ALU.mult), waits=[trc])
        tinv = p.op("vector", lambda h: h.tensor_tensor(A_dS[64:128, :], A_dS[64:128, :],
                                                        A_tS[64:128, :], ALU.mult), waits=[tcp])

        tkabev = {}
        for kc in range(8):
            w = [tlh, tinv] + ([tkabev[kc - 2]] if kc >= 2 else [psA[0], psA[1]])
            tmm = p.op("tensor", lambda h, kc=kc: h.matmul(P[kc % 2][0:8, :], LT[:],
                       A_dS[:, kc * 512:(kc + 1) * 512], start=True, stop=True), waits=w, sem=TM)
            tkabev[kc] = p.op("scalar", lambda h, kc=kc: h.copy(A_kab[:, kc * 512:(kc + 1) * 512],
                                                                P[kc % 2][0:8, :]), waits=[tmm])
        tkh = p.op("gpsimd", lambda h: h.dma_start(out=kabh.rearrange("r p f -> r (p f)"), in_=A_kab),
                   waits=[tkabev[kc] for kc in range(8)], sem=KP, dma=True)
        for r, nm in enumerate(fan_names):
            p.op("gpsimd", lambda h, r=r, nm=nm: h.dma_start(out=fanmap[nm], in_=kabh[r]),
                 waits=[tkh], sem=KP, dma=True)
        tfan = [(KP, KP["n"])]

        k00r, k01r, k10r, k11r = fanmap["k00r"], fanmap["k01r"], fanmap["k10r"], fanmap["k11r"]
        k00i, k01i, k10i, k11i = fanmap["k00i"], fanmap["k01i"], fanmap["k10i"], fanmap["k11i"]
        w_re, w_imP = mids["w_re"], mids["w_imP"]
        n_re, n_im = mids["n_re"], mids["n_im"]
        s_re, s_im = mids["s_re"], mids["s_im"]
        tm1f, tm2f, tm3f = mids["tm1"], mids["tm2"], mids["tm3"]
        p.op("vector", lambda h: h.tensor_scalar(tm1f, k11r, 1.0, None, ALU.add),
             waits=tfan + [tload["c_re"], tload["c_im"]])
        p.op("vector", lambda h: h.tensor_tensor(tm2f, tm1f, tm1f, ALU.mult))
        p.op("vector", lambda h: h.tensor_tensor(tm3f, k11i, k11i, ALU.mult))
        p.op("vector", lambda h: h.tensor_tensor(tm2f, tm2f, tm3f, ALU.add))
        recip_nr(w_imP, tm2f, tm3f)
        p.op("vector", lambda h: h.tensor_copy(tm2f, w_imP))
        p.op("vector", lambda h: h.tensor_tensor(w_re, tm1f, tm2f, ALU.mult))
        p.op("vector", lambda h: h.tensor_tensor(w_imP, k11i, tm2f, ALU.mult))
        p.op("vector", lambda h: h.tensor_tensor(tm1f, k01r, k10r, ALU.mult))
        p.op("vector", lambda h: h.tensor_tensor(tm2f, k01i, k10i, ALU.mult))
        p.op("vector", lambda h: h.tensor_tensor(n_re, tm1f, tm2f, ALU.subtract))
        p.op("vector", lambda h: h.tensor_tensor(tm1f, k01r, k10i, ALU.mult))
        p.op("vector", lambda h: h.tensor_tensor(tm2f, k01i, k10r, ALU.mult))
        p.op("vector", lambda h: h.tensor_tensor(n_im, tm1f, tm2f, ALU.add))
        p.op("vector", lambda h: h.tensor_tensor(tm1f, n_re, w_re, ALU.mult))
        p.op("vector", lambda h: h.tensor_tensor(tm2f, n_im, w_imP, ALU.mult))
        p.op("vector", lambda h: h.tensor_tensor(tm3f, tm1f, tm2f, ALU.add))
        p.op("vector", lambda h: h.tensor_tensor(s_re, k00r, tm3f, ALU.subtract))
        p.op("vector", lambda h: h.tensor_tensor(tm1f, n_im, w_re, ALU.mult))
        p.op("vector", lambda h: h.tensor_tensor(tm2f, n_re, w_imP, ALU.mult))
        p.op("vector", lambda h: h.tensor_tensor(tm3f, tm1f, tm2f, ALU.subtract))
        p.op("vector", lambda h: h.tensor_tensor(s_im, k00i, tm3f, ALU.subtract))
        p.op("vector", lambda h: h.tensor_tensor(tm1f, sb["c_re"][:], s_re, ALU.mult))
        p.op("vector", lambda h: h.tensor_tensor(tm2f, sb["c_im"][:], s_im, ALU.mult))
        p.op("vector", lambda h: h.tensor_tensor(A_aS[0:64, :], tm1f, tm2f, ALU.subtract))
        p.op("vector", lambda h: h.tensor_tensor(tm1f, sb["c_re"][:], s_im, ALU.mult))
        p.op("vector", lambda h: h.tensor_tensor(tm2f, sb["c_im"][:], s_re, ALU.mult))
        tat = p.op("vector", lambda h: h.tensor_tensor(A_aS[64:128, :], tm1f, tm2f, ALU.add))

        ts1 = p.op("tensor", lambda h: h.matmul(P[2][0:64, 0:64], A_aS, sb["Es1re"][:], start=True, stop=True),
                   waits=[tat, tload["Es1re"], tload["Es1im"], t_sbc_done], sem=TM)
        ts2m = p.op("tensor", lambda h: h.matmul(P[3][0:64, 0:64], A_aS, sb["Es1im"][:], start=True, stop=True),
                    sem=TM)
        st_re, st_im = mids["st_re"], mids["st_im"]
        te1 = p.op("scalar", lambda h: h.copy(st_re, P[2][0:64, 0:64]), waits=[ts1])
        te2 = p.op("scalar", lambda h: h.copy(st_im, P[3][0:64, 0:64]), waits=[ts2m])
        p.op("vector", lambda h: h.tensor_tensor(tm1f, st_re, sb["tw_re"][:], ALU.mult),
             waits=[te1, te2, tload["tw_re"], tload["tw_im"]])
        p.op("vector", lambda h: h.tensor_tensor(tm2f, st_im, sb["tw_im"][:], ALU.mult))
        p.op("vector", lambda h: h.tensor_tensor(A_TS[0:64, :], tm1f, tm2f, ALU.subtract))
        p.op("vector", lambda h: h.tensor_tensor(tm1f, st_re, sb["tw_im"][:], ALU.mult))
        p.op("vector", lambda h: h.tensor_tensor(tm2f, st_im, sb["tw_re"][:], ALU.mult))
        ttw = p.op("vector", lambda h: h.tensor_tensor(A_TS[64:128, :], tm1f, tm2f, ALU.add))
        ts3 = p.op("tensor", lambda h: h.matmul(P[2][0:64, 0:64], sb["Es3"][:], A_TS, start=True, stop=True),
                   waits=[ttw, tload["Es3"], te1], sem=TM)
        tk1 = p.op("scalar", lambda h: h.copy(sb["Ksb"], P[2][0:64, 0:64]), waits=[ts3])
        tkD = p.op("scalar", lambda h: h.activation(sb["Ksb"][0:1, 0:1], sb["Dsc"][:], AF.Identity,
                                                    bias=sb["Ksb"][0:1, 0:1], scale=1.0),
                   waits=[tk1, tload["Dsc"]])
        tz = p.op("vector", lambda h: h.memset(sb["tmp"][0:1, 0:PAD], 0.0))
        tz1 = p.op("gpsimd", lambda h: h.dma_start(out=kpad[0:PAD].rearrange("(a b) -> a b", a=1),
                                                   in_=sb["tmp"][0:1, 0:PAD]), waits=[tz], sem=KZ, dma=True)
        tz2 = p.op("gpsimd", lambda h: h.dma_start(out=kpad[PAD:PAD + L].rearrange("(a b) -> a b", a=64),
                                                   in_=sb["Ksb"]), waits=[tkD], sem=KZ, dma=True)
        tkz_all = (KZ, KZ["n"])

        psA[0], psA[1], psA[2], psA[3] = tkabev[6], tkabev[7], tk1, te2
        psB = [t_g2, tBr, tBi, None]
        G46 = [tat, ttw, tinv, ts1, ts2m, ts3, t9, t10, t1, t2, ALL_LOADS]
        slot_guards = {0: [tkh], 1: [tkh], 2: [tkh], 3: [tkh],
                       4: G46, 5: G46, 6: G46,
                       7: [t5, t6, ts2, t4, tkz_all, ALL_LOADS], 8: []}
        slot_first = [True] * NSLOT
        t_sh_store = None

        slot_free = [None] * NSLOT
        slot_prev = [None] * NSLOT
        dslot_free = [None, None]
        dslot_prev = [None, None]
        win_prev = [None] * 4
        xp_prev = [None, None]
        xp_guard = [None, None]
        tmp_guard = [tkz_all, tkz_all]

        ssm_sb_free = [[None] * 4, [None] * 4]
        gsb_free = [t_sbc_done] * 4
        prev_chunk_ssm_done = None
        prev_strip_load = None
        prev_strip_ev = None
        gsb_prev = [None] * 4
        ln_apply_tok = {}
        last_gu_read = None
        t_arena_free = None
        slot_idx = 0
        t_out_all = []

        for cnk in range(NCH):
            t_ln_done = t_ln_done0 if cnk == 0 else ln_apply_tok[cnk]

            tok_gu_w = {}
            arena_guard = [prev_chunk_ssm_done, prev_strip_load]
            for gi, (ib0, gw) in enumerate(IGROUPS):
                for mat in ("g", "u"):
                    psum_set = PA if mat == "g" else PB
                    pf = psA if mat == "g" else psB
                    tok_piece = {}
                    slot_of = {}
                    for q in range(8):
                        sl = slot_idx % NSLOT
                        slot_idx += 1
                        wpd = [slot_free[sl], slot_prev[sl]] + arena_guard
                        if cnk == 0 and slot_first[sl]:
                            wpd += slot_guards[sl]
                            slot_first[sl] = False
                        src_ap = (D[mat + "main"][gi, q] if gw == 4 else D[mat + "tail"][q])
                        dst = sb["arena"][:, sl * 2048: sl * 2048 + 4 * gw * 128]
                        if mat == "g":
                            tok_piece[q] = p.op("sync", lambda h, src_ap=src_ap, dst=dst: h.dma_start(
                                out=dst.bitcast(U16), in_=src_ap.rearrange("p hb i -> p (hb i)")),
                                waits=wpd, sem=WSL[sl], dma=True)
                        else:
                            tok_piece[q] = p.op("gpsimd", lambda h, src_ap=src_ap, dst=dst: h.dma_start(
                                out=dst, in_=src_ap.rearrange("p hb i -> p (hb i)")),
                                waits=wpd, sem=WSL[sl], dma=True)
                        slot_prev[sl] = tok_piece[q]
                        slot_of[q] = sl
                    for j in range(gw):
                        tbank = None
                        for hb in range(32):
                            q = hb // 4
                            sl = slot_of[q]
                            give_tok = (hb == 31) or (j == gw - 1 and hb % 4 == 3)
                            wmm = []
                            if hb == 0:
                                wmm = [pf[j]]
                                if j == 0:
                                    wmm += [tok_piece[0], t_ln_done]
                            elif hb % 4 == 0 and j == 0:
                                wmm = [tok_piece[q]]
                            tmm = p.op("tensor", lambda h, sl=sl, hb=hb, j=j, gw=gw, pj=psum_set[j]: h.matmul(
                                pj[:],
                                sb["arena"][:, sl * 2048 + (hb % 4) * (gw * 128) + j * 128:
                                            sl * 2048 + (hb % 4) * (gw * 128) + (j + 1) * 128],
                                sb["xnT"][:, hb * 512:(hb + 1) * 512],
                                start=(hb == 0), stop=(hb == 31)),
                                waits=wmm, sem=TM if give_tok else None)
                            if give_tok:
                                if hb == 31:
                                    tbank = tmm
                                if j == gw - 1 and hb % 4 == 3:
                                    slot_free[sl] = tmm
                        if mat == "g":
                            tgc = p.op("scalar", lambda h, j=j: h.copy(g_sb[j], PA[j][:]),
                                       waits=[tbank, gsb_free[j], dslot_prev[j // 2], dslot_free[j // 2], tds_kp])
                            gsb_prev[j] = tgc
                            psA[j] = tgc
                            tok_gu_w[("gsb", gi, j)] = tgc
                        else:
                            ib = ib0 + j
                            wst = [tbank, tok_gu_w[("gsb", gi, j)], ts2]
                            if cnk == 0 and ib < 64:
                                wst.append(tkabev[7])
                            tst = p.op("vector", lambda h, ib=ib, j=j: h.scalar_tensor_tensor(
                                sb["gu"][:, ib * 512:(ib + 1) * 512], PB[j][:],
                                sb["s2"][:, ib:ib + 1], g_sb[j], ALU.mult, ALU.mult), waits=wst)
                            psB[j] = tst
                            gsb_free[j] = tst
                            tok_gu_w[ib] = tst
            t_gu_last = tok_gu_w[NIB - 1]
            if cnk + 1 < NCH:
                emit_ln_stats(cnk + 1, [0, 1, 2, 3])

            if cnk == 0:
                t_strip0 = {}
                win_prev0 = [None] * 4

                def win_dma0(h, off, wb):
                    w = kpad.copy()
                    w.offset = off
                    w.ap[0] = [1, 128]
                    w.ap.append([1, 512])
                    return h.dma_start(out=wb, in_=w)
                for s, dd in enumerate(range(-3, 29)):
                    off = PAD - 127 + 128 * dd
                    wb = wbufB[s % 4]
                    wwin = [tkz_all, t_gu_last, win_prev0[s % 4]] + ([t_strip0[s - 4]] if s >= 4 else [])
                    twn = p.op("gpsimd", lambda h, off=off, wb=wb: win_dma0(h, off, wb),
                               waits=wwin, sem=WIN[s % 4], dma=True)
                    win_prev0[s % 4] = twn
                    wrm = [twn, trv_bf, psA[s % 2], t_gu_last]
                    tsm = p.op("tensor", lambda h, s=s, wb=wb: h.matmul(P[s % 2][:], sb["Rrev_bf"][:], wb,
                               start=True, stop=True), waits=wrm, sem=TM)
                    t_strip0[s] = p.op("scalar", lambda h, s=s: h.copy(
                        sb["arena"][:, s * 512:(s + 1) * 512], P[s % 2][:]),
                        waits=[tsm, slot_prev[s // 4]])
                    psA[s % 2] = t_strip0[s]
                t_sh_store = p.op("gpsimd", lambda h: h.dma_start(out=striph, in_=sb["arena"][:, 0:16384]),
                                  waits=[t_strip0[s] for s in range(32)], sem=SH, dma=True)


                t_sload = t_strip0[31]
            else:
                t_sload = p.op("gpsimd", lambda h: h.dma_start(out=sb["arena"][:, 0:16384], in_=striph),
                               waits=[t_gu_last, prev_strip_load] + [slot_prev[i] for i in range(NSLOT)]
                                     + [slot_free[i] for i in range(NSLOT)],
                               sem=SH, dma=True)
            prev_strip_load = t_sh_store if cnk == 0 else t_sload
            for i in range(8):
                slot_prev[i] = prev_strip_load

            ssm_par = [sb["ssm0"], sb["ssm1"]]
            t_ssm_ev = {}
            last_ssm_mm = None
            last_down_mm = None
            for c in range(8):
                par = c % 2
                Ilist = list(range(0, min(32, 4 * c + 4)))
                for tb in range(4):
                    tmm = None
                    for ix, I in enumerate(Ilist):
                        sidx = 4 * c - I + 3
                        last = (ix == len(Ilist) - 1)
                        wss = []
                        if ix == 0:
                            wss = [t_sload, psA[tb]]
                        tmm = p.op("tensor", lambda h, I=I, tb=tb, sidx=sidx, first=(ix == 0), last=last:
                                   h.matmul(PA[tb][:],
                                            sb["xnT"][:, I * 512 + tb * 128: I * 512 + (tb + 1) * 128],
                                            sb["arena"][:, sidx * 512:(sidx + 1) * 512],
                                            start=first, stop=last),
                                   waits=wss, sem=TM if last else None)
                    last_ssm_mm = tmm
                    tse = p.op("scalar", lambda h, par=par, tb=tb: h.copy(
                        ssm_par[par][:, tb * 512:(tb + 1) * 512], PA[tb][:]),
                        waits=[tmm, ssm_sb_free[par][tb]])
                    t_ssm_ev[(c, tb)] = tse
                    psA[tb] = tse
                if c == 7 and cnk + 1 < NCH:
                    ln_apply_tok[cnk + 1] = emit_ln_apply(cnk + 1, [0, 1, 2, 3])
                txp_c = {}
                for tb in range(2):
                    row0 = cnk * CH + tb * 128
                    sl2 = tb % 2
                    txp_c[tb] = p.op("scalar", lambda h, row0=row0, c=c, sl2=sl2: h.dma_start(
                        out=sb["xp"][:, sl2 * 512:(sl2 + 1) * 512],
                        in_=D["x"][row0:row0 + 128, c * 512:(c + 1) * 512]),
                        waits=[xp_guard[sl2], xp_prev[sl2]], sem=XP[sl2], dma=True)
                    xp_prev[sl2] = txp_c[tb]
                for jj, (ib0, gw4) in enumerate(IGROUPS):
                    dsl_i = jj % 2
                    wdp = [dslot_free[dsl_i], dslot_prev[dsl_i]]
                    if c == 0 and jj < 2:
                        wdp += [gsb_free[j] for j in range(4)] + [gsb_prev[j] for j in range(4)]
                    src = (D["dmain"][c, jj] if gw4 == 4 else D["dtail"][c])
                    dstp = sb["dring"][:, dsl_i * 2048: dsl_i * 2048 + gw4 * 512]
                    tdp = p.op("sync", lambda h, src=src, dstp=dstp: h.dma_start(
                        out=dstp.bitcast(U16), in_=src.rearrange("p j hw -> p (j hw)")),
                        waits=wdp, sem=DSL[dsl_i], dma=True)
                    dslot_prev[dsl_i] = tdp
                    tgrp = None
                    for j in range(gw4):
                        ib = ib0 + j
                        for tb in range(4):
                            wmm = []
                            if j == 0 and tb == 0:
                                wmm.append(tdp)
                            if ib == 0:
                                wmm += [psB[tb], t_gu_last]
                            give = (j == gw4 - 1 and tb == 3)
                            tmm = p.op("tensor", lambda h, ib=ib, tb=tb, dsl_i=dsl_i, j=j: h.matmul(
                                PB[tb][:],
                                sb["gu"][:, ib * 512 + tb * 128: ib * 512 + (tb + 1) * 128],
                                sb["dring"][:, dsl_i * 2048 + j * 512: dsl_i * 2048 + (j + 1) * 512],
                                start=(ib == 0), stop=(ib == NIB - 1)),
                                waits=wmm, sem=TM if give else None)
                            if give:
                                tgrp = tmm
                    dslot_free[dsl_i] = tgrp
                    if ib0 + gw4 == NIB:
                        last_down_mm = tgrp
                for tb in range(4):
                    row0 = cnk * CH + tb * 128
                    sl2 = tb % 2
                    if tb >= 2:
                        txp_c[tb] = p.op("scalar", lambda h, row0=row0, c=c, sl2=sl2: h.dma_start(
                            out=sb["xp"][:, sl2 * 512:(sl2 + 1) * 512],
                            in_=D["x"][row0:row0 + 128, c * 512:(c + 1) * 512]),
                            waits=[xp_guard[sl2], xp_prev[sl2]], sem=XP[sl2], dma=True)
                        xp_prev[sl2] = txp_c[tb]
                    txp = txp_c[tb]
                    tv1 = p.op("vector", lambda h, tb=tb, c=c, sl2=sl2: h.tensor_tensor(
                        sb["tmp"][:, sl2 * 512:(sl2 + 1) * 512], PB[tb][:],
                        sb["s_bc"][:, c * 512:(c + 1) * 512], ALU.mult),
                        waits=[last_down_mm, t_sbc_done, tmp_guard[sl2]])
                    tv2 = p.op("vector", lambda h, tb=tb, par=par, sl2=sl2: h.tensor_tensor(
                        sb["tmp"][:, sl2 * 512:(sl2 + 1) * 512],
                        sb["tmp"][:, sl2 * 512:(sl2 + 1) * 512],
                        ssm_par[par][:, tb * 512:(tb + 1) * 512], ALU.add),
                        waits=[t_ssm_ev[(c, tb)]])
                    tv3 = p.op("vector", lambda h, tb=tb, sl2=sl2: h.tensor_tensor(
                        sb["tmp"][:, sl2 * 512:(sl2 + 1) * 512],
                        sb["tmp"][:, sl2 * 512:(sl2 + 1) * 512],
                        sb["xp"][:, sl2 * 512:(sl2 + 1) * 512], ALU.add),
                        waits=[txp])
                    psB[tb] = tv1
                    ssm_sb_free[par][tb] = tv2
                    xp_guard[sl2] = tv3
                    tst_o = p.op("sync", lambda h, row0=row0, c=c, sl2=sl2: h.dma_start(
                        out=OUT[row0:row0 + 128, c * 512:(c + 1) * 512],
                        in_=sb["tmp"][:, sl2 * 512:(sl2 + 1) * 512]),
                        waits=[tv3], sem=STS[sl2], dma=True)
                    tmp_guard[sl2] = tst_o
                    t_out_all.append(tst_o)
            prev_chunk_ssm_done = last_ssm_mm
            last_gu_read = last_down_mm
            t_arena_free = last_ssm_mm

        p.op("sync", lambda h: h.nop(), waits=[(STS[0], STS[0]["n"]), (STS[1], STS[1]["n"])])

        p.replay(block)
    return nc


def kernel(L0_attn_u, L0_attn_Lambda_ri, L0_attn_p_ri, L0_attn_q_ri, L0_attn_Vc_ri,
           L0_attn_Ct_ri, L0_attn_Bp, L0_attn_D, L0_attn_log_step,
           L0_ffn_gate_wq, L0_ffn_gate_scale, L0_ffn_up_wq, L0_ffn_up_scale,
           L0_ffn_down_wq, L0_ffn_down_scale):
    tbl = host_tables()
    wst = stage_weights(np.asarray(L0_ffn_gate_wq), np.asarray(L0_ffn_up_wq),
                        np.asarray(L0_ffn_down_wq))
    xfull = np.ascontiguousarray(np.asarray(L0_attn_u, dtype=np.float32).reshape(8192, L))

    common = dict(
        Lam=np.ascontiguousarray(np.asarray(L0_attn_Lambda_ri, np.float32)),
        pri=np.ascontiguousarray(np.asarray(L0_attn_p_ri, np.float32)),
        qri=np.ascontiguousarray(np.asarray(L0_attn_q_ri, np.float32)),
        Vc=np.ascontiguousarray(np.asarray(L0_attn_Vc_ri, np.float32).reshape(64, 128)),
        Ct=np.ascontiguousarray(np.asarray(L0_attn_Ct_ri, np.float32).reshape(64, 2)),
        Bp=np.ascontiguousarray(np.asarray(L0_attn_Bp, np.float32)),
        Dsc=np.ascontiguousarray(np.asarray(L0_attn_D, np.float32).reshape(1, 1)),
        lstep=np.ascontiguousarray(np.asarray(L0_attn_log_step, np.float32).reshape(1, 1)),
        gs=np.ascontiguousarray(np.asarray(L0_ffn_gate_scale, np.float32).reshape(86, 128)),
        us=np.ascontiguousarray(np.asarray(L0_ffn_up_scale, np.float32).reshape(86, 128)),
        ds=np.ascontiguousarray(np.asarray(L0_ffn_down_scale, np.float32).reshape(1, L)),
        gmain=wst["gmain"], gtail=wst["gtail"],
        umain=wst["umain"], utail=wst["utail"],
        dmain=wst["dmain"], dtail=wst["dtail"],
        **tbl,
    )
    in_maps = []
    for c in range(NCORE):
        m = dict(common)
        m["x"] = np.ascontiguousarray(xfull[c * TOKC:(c + 1) * TOKC])
        in_maps.append(m)

    nc = build_program()
    res = run_bass_kernel_spmd(nc, in_maps, core_ids=list(range(NCORE)))
    out = np.concatenate([np.asarray(r["o"], np.float32) for r in res.results], 0)
    return out.reshape(2, 4096, L)


# version 15
# speedup vs baseline: 1.0812x; 1.0569x over previous
import numpy as np
import ml_dtypes
import contextlib
import concourse.bass as bass
import concourse.mybir as mybir
from concourse.bass_utils import run_bass_kernel_spmd
from concourse.alu_op_type import AluOpType as ALU
import concourse.bass_utils as _bu

if not getattr(_bu, "_ldw_opt_patched", False):
    _orig_run_command = _bu.run_command

    def _run_command_ldw(cmd, *a, **kw):
        cmd = [("--enable-ldw-opt=true" if c == "--enable-ldw-opt=false" else c) for c in cmd]
        return _orig_run_command(cmd, *a, **kw)

    _bu.run_command = _run_command_ldw
    _bu._ldw_opt_patched = True

F32 = mybir.dt.float32
BF16 = mybir.dt.bfloat16
I8 = mybir.dt.int8
U16 = mybir.dt.uint16
AF = mybir.ActivationFunctionType
AX = mybir.AxisListType

L = 4096
NCORE = 8
TOKC = 1024
CH = 512
NCH = TOKC // CH
NIB = 86
IGROUPS = [(g * 4, 4) for g in range(21)] + [(84, 2)]
PAD = 512
NSLOT = 9
ENGINES = ("sync", "gpsimd", "vector", "scalar", "tensor")


class Prog:

    def __init__(self, nc, es):
        self.nc = nc
        self.stream = {e: [] for e in ENGINES}
        self.sems = {}
        self.es = es

    def sem(self, name):
        h = self.es.enter_context(self.nc.semaphore(name))
        s = {"h": h, "n": 0, "name": name}
        self.sems[name] = s
        return s

    def op(self, eng, fn, waits=(), sem=None, dma=False):
        waits = [w for w in waits if w is not None]
        if eng in ("vector", "scalar") and not dma:
            chain = self.sems[eng.upper() + "CH"]
            if chain["n"] > 0:
                waits.append((chain, chain["n"]))
            assert sem is None or sem is chain
            sem = chain
            sem["n"] += 1
            self.stream[eng].append((fn, waits, sem, 1))
            return (sem, sem["n"])
        tok = None
        if sem is not None:
            sem["n"] += 16 if dma else 1
            tok = (sem, sem["n"])
        self.stream[eng].append((fn, waits, sem, 16 if dma else 1))
        return tok

    def replay(self, block):
        for eng in ENGINES:
            stream = self.stream[eng]
            if not stream:
                continue

            def make(stream):
                def body(h):
                    hw = {}
                    for fn, waits, sem, inc in stream:
                        for s, v in waits:
                            if hw.get(id(s), -1) < v:
                                h.wait_ge(s["h"], v)
                                hw[id(s)] = v
                        ins = fn(h)
                        if sem is not None:
                            ins.then_inc(sem["h"], inc)
                return body
            getattr(block, eng)(make(stream))


def host_tables():
    f32, f64 = np.float32, np.float64
    k = np.arange(L)
    x = (k / L).astype(f32)
    th = (x * np.float32(-2 * np.pi)).astype(f32)
    O = np.cos(th.astype(f64)).astype(f32).astype(f64) \
        + 1j * np.sin(th.astype(f64)).astype(f32).astype(f64)
    c = 2.0 / (1.0 + O)
    t = (1.0 - O) / (1.0 + O)
    n = np.arange(64)
    E = np.exp(2j * np.pi * np.outer(n, n) / 64.0)
    tw = np.exp(2j * np.pi * np.outer(n, n) / float(L))
    R = np.zeros((128, 128), f32)
    R[np.arange(128), 127 - np.arange(128)] = 1.0
    return {
        "c_re": np.ascontiguousarray(c.real.astype(f32).reshape(64, 64)),
        "c_im": np.ascontiguousarray(c.imag.astype(f32).reshape(64, 64)),
        "tS": np.ascontiguousarray(np.concatenate(
            [np.broadcast_to(t.real.astype(f32), (64, L)),
             np.broadcast_to(t.imag.astype(f32), (64, L))], 0)),
        "Es1re": np.ascontiguousarray(np.concatenate([E.real, -E.imag], 0).astype(f32)),
        "Es1im": np.ascontiguousarray(np.concatenate([E.imag, E.real], 0).astype(f32)),
        "Es3": np.ascontiguousarray(np.concatenate([E.real / L, -E.imag / L], 0).astype(f32)),
        "tw_re": np.ascontiguousarray(tw.real.astype(f32)),
        "tw_im": np.ascontiguousarray(tw.imag.astype(f32)),
        "Rrev": R,
        "ident_f": np.eye(128, dtype=f32),
        "ones_row": np.ones((1, 128), f32),
        "exp_lo": (np.arange(76) / 16.0 - 6.9375 - 1.0 / 32).astype(f32).reshape(1, 76),
        "exp_h": (np.arange(76) / 16.0 - 6.9375).astype(f32).reshape(1, 76),
        "exp_e": np.exp(np.arange(76) / 16.0 - 6.9375).astype(f32).reshape(1, 76),
    }


def stage_weights(gate_wq, up_wq, down_wq):
    def i8(w):
        return np.ascontiguousarray(w).view(np.int8)[:, ::4]

    out = {}
    for name, wq in (("g", gate_wq), ("u", up_wq)):
        wT = i8(wq).T
        out[name + "main"] = np.ascontiguousarray(
            wT[:, :10752].reshape(8, 4, 128, 21, 512).transpose(3, 0, 2, 1, 4))
        out[name + "tail"] = np.ascontiguousarray(
            wT[:, 10752:].reshape(8, 4, 128, 256).transpose(0, 2, 1, 3))
    dT = i8(down_wq).T.astype(ml_dtypes.bfloat16).view(np.uint16)
    out["dmain"] = np.ascontiguousarray(
        dT[:10752, :].reshape(21, 4, 128, 8, 512).transpose(3, 0, 2, 1, 4))
    out["dtail"] = np.ascontiguousarray(
        dT[10752:, :].reshape(2, 128, 8, 512).transpose(2, 1, 0, 3))
    return out


def build_program():
    nc = bass.Bass("TRN2", target_bir_lowering=False)
    D = {}
    dram_spec = [
        ("x", [TOKC, L], F32),
        ("Lam", [64, 2], F32), ("pri", [64, 2], F32), ("qri", [64, 2], F32),
        ("Vc", [64, 128], F32), ("Ct", [64, 2], F32), ("Bp", [64, 1], F32),
        ("Dsc", [1, 1], F32), ("lstep", [1, 1], F32),
        ("gs", [86, 128], F32), ("us", [86, 128], F32), ("ds", [1, L], F32),
        ("gmain", [21, 8, 128, 4, 512], I8), ("gtail", [8, 128, 4, 256], I8),
        ("umain", [21, 8, 128, 4, 512], I8), ("utail", [8, 128, 4, 256], I8),
        ("dmain", [8, 21, 128, 4, 512], U16), ("dtail", [8, 128, 2, 512], U16),
        ("c_re", [64, 64], F32), ("c_im", [64, 64], F32), ("tS", [128, L], F32),
        ("Es1re", [128, 64], F32), ("Es1im", [128, 64], F32), ("Es3", [128, 64], F32),
        ("tw_re", [64, 64], F32), ("tw_im", [64, 64], F32),
        ("Rrev", [128, 128], F32), ("ident_f", [128, 128], F32),
        ("ones_row", [1, 128], F32),
        ("exp_lo", [1, 76], F32), ("exp_h", [1, 76], F32), ("exp_e", [1, 76], F32),
    ]
    for name, shape, dt in dram_spec:
        D[name] = nc.dram_tensor(name, shape, dt, kind="ExternalInput").ap()
    OUT = nc.dram_tensor("o", [TOKC, L], F32, kind="ExternalOutput").ap()
    kpad = nc.dram_tensor("kpad", [PAD + L], F32, kind="Internal").ap()
    striph = nc.dram_tensor("striph", [128, 32 * 512], BF16, kind="Internal").ap()
    kabh = nc.dram_tensor("kabh", [8, 64, 64], F32, kind="Internal").ap()

    es = contextlib.ExitStack()
    with es:
        sb = {}

        def S(name, shape, dt=F32):
            sb[name] = es.enter_context(nc.sbuf_tensor("sb_" + name, shape, dt))
            return sb[name]

        S("xnT", [128, 32 * CH], BF16)
        S("gu", [128, NIB * CH], BF16)
        S("arena", [128, NSLOT * 2048], BF16)
        S("dring", [128, 2 * 2048], BF16)
        S("ssm0", [128, 4 * CH], BF16)
        S("ssm1", [128, 4 * CH], BF16)
        S("s_bc", [128, L], BF16)
        S("tmp", [128, 2 * CH], F32)
        S("xp", [128, 2 * CH], F32)
        S("xq", [128, 2048], F32)
        S("xnb", [128, L], BF16)
        S("ident_bf", [128, 128], BF16)
        S("Rrev_bf", [128, 128], BF16)
        S("Rrev", [128, 128], F32)
        S("ones_row", [1, 128], F32)
        S("s2", [128, 88], F32)
        S("stats", [128, 48], F32)
        S("mv", [128, 2], F32)
        S("rstdT", [128, 4], F32)
        S("nmrT", [128, 4], F32)
        for nm, shp in [("Lam", [64, 2]), ("pri", [64, 2]), ("qri", [64, 2]),
                        ("Ct", [64, 2]), ("Bp", [64, 1]),
                        ("Dsc", [1, 1]), ("lstep", [1, 1]),
                        ("c_re", [64, 64]), ("c_im", [64, 64]),
                        ("Es1re", [128, 64]), ("Es1im", [128, 64]), ("Es3", [128, 64]),
                        ("tw_re", [64, 64]), ("tw_im", [64, 64]),
                        ("step2", [1, 1]), ("g2x", [128, 1]), ("lamS", [128, 1]), ("neglam", [128, 1]),
                        ("lhsTS", [128, 8]),
                        ]:
            S(nm, shp)

        guf = sb["gu"][:].bitcast(F32)
        A_tS = guf[:, 0:4096]
        A_dS = guf[:, 4096:8192]
        A_mag = guf[0:64, 8192:12288]
        A_nrt = guf[0:64, 12288:16384]
        arf = sb["arena"][:].bitcast(F32)
        A_kab = arf[0:8, 0:4096]
        fan_names = ["k00r", "k01r", "k10r", "k11r", "k00i", "k01i", "k10i", "k11i"]
        fanmap = {nm: arf[0:64, 4096 + i * 64: 4096 + (i + 1) * 64]
                  for i, nm in enumerate(fan_names)}
        mids = {nm: arf[0:64, 4608 + i * 64: 4608 + (i + 1) * 64]
                for i, nm in enumerate(["w_re", "w_imP", "n_re", "n_im", "s_re", "s_im",
                                        "tm1", "tm2", "tm3", "st_re", "st_im"])}
        A_aS = arf[:, 5312:5376]
        A_TS = arf[:, 5376:5440]
        A_vt = arf[0:64, 5440:5568]
        A_B = arf[0:64, 5568:5570]
        A_exp = arf[0:1, 5632:6400]
        wbufB = [sb["arena"][:, 16384 + i * 512:16384 + (i + 1) * 512] for i in range(4)]
        sb["Vc"] = arf[0:64, 7168:7296]
        sb["ident_f"] = arf[:, 7296:7424]
        sb["s2b"] = arf[:, 7424:7512]
        sb["Ksb"] = arf[0:64, 7512:7576]
        sb["exp_lo"] = arf[0:1, 6400:6476]
        sb["exp_h"] = arf[0:1, 6528:6604]
        sb["exp_e"] = arf[0:1, 6656:6732]
        sb["gsl"] = arf[0:86, 6784:6912]
        sb["usl"] = arf[0:86, 6912:7040]

        gsbf = sb["dring"][:].bitcast(F32)
        g_sb = [gsbf[:, j * 512:(j + 1) * 512] for j in range(4)]

        P = [es.enter_context(nc.psum_tensor(f"P{i}", [128, 512], F32)) for i in range(8)]
        PA, PB = P[0:4], P[4:8]

        p = Prog(nc, es)
        LD = p.sem("LD")
        KP = p.sem("KP")
        KZ = p.sem("KZ")
        XT = p.sem("XT")
        XP = [p.sem("XP0"), p.sem("XP1")]
        STS = [p.sem("ST0"), p.sem("ST1")]
        WIN = [p.sem(f"WIN{i}") for i in range(4)]
        XQ = [p.sem("XQ0"), p.sem("XQ1")]
        SH = p.sem("SH")
        WSL = [p.sem(f"W{i}") for i in range(NSLOT)]
        DSL = [p.sem("D0"), p.sem("D1")]
        p.sem("VECTORCH")
        p.sem("SCALARCH")
        TM = p.sem("TM")
        block = es.enter_context(nc.Block())

        psA = [None, None, None, None]

        def recip_nr(dst, src_ap, tmp_ap, extra_waits=()):
            p.op("vector", lambda h: h.reciprocal(dst, src_ap), waits=extra_waits)
            p.op("vector", lambda h: h.tensor_tensor(tmp_ap, src_ap, dst, ALU.mult))
            p.op("vector", lambda h: h.tensor_scalar(tmp_ap, tmp_ap, -1.0, 2.0, ALU.mult, ALU.add))
            return p.op("vector", lambda h: h.tensor_tensor(dst, dst, tmp_ap, ALU.mult))

        ln = {"xq_guard": [None, None], "xq_prev": [None, None], "xnb_guard": None}

        def emit_ln_stats(cnk, tiles):
            tok = None
            for tt in tiles:
                row0 = cnk * CH + tt * 128
                tbs = None
                for qd in range(4):
                    slq = qd % 2
                    xqs = sb["xq"][:, slq * 1024:(slq + 1) * 1024]
                    tql = p.op("sync", lambda h, row0=row0, qd=qd, xqs=xqs: h.dma_start(
                        out=xqs, in_=D["x"][row0:row0 + 128, qd * 1024:(qd + 1) * 1024]),
                        waits=[ln["xq_guard"][slq], ln["xq_prev"][slq]], sem=XQ[slq], dma=True)
                    ln["xq_prev"][slq] = tql
                    for half in range(2):
                        tbs = p.op("vector", lambda h, qd=qd, half=half, xqs=xqs: h.bn_stats(
                            sb["stats"][:, (qd * 2 + half) * 6:(qd * 2 + half + 1) * 6],
                            xqs[:, half * 512:(half + 1) * 512]),
                            waits=[tql] if half == 0 else ())
                    ln["xq_guard"][slq] = tbs
                p.op("vector", lambda h: h.bn_aggr(sb["mv"][:], sb["stats"][:]))
                rc = sb["rstdT"][:, tt:tt + 1]
                tve = p.op("vector", lambda h, rc=rc: h.tensor_scalar(rc, sb["mv"][:, 1:2],
                                                                      1.0, 1e-5, ALU.mult, ALU.add))
                p.op("scalar", lambda h, rc=rc: h.activation(rc, rc, AF.Ln), waits=[tve])
                tl2 = p.op("scalar", lambda h, rc=rc: h.activation(rc, rc, AF.Exp, scale=-0.5))
                tok = p.op("vector", lambda h, tt=tt, rc=rc: h.scalar_tensor_tensor(
                    sb["nmrT"][:, tt:tt + 1], sb["mv"][:, 0:1], -1.0, rc, ALU.mult, ALU.mult),
                    waits=[tl2])
                ln[("nmr", cnk, tt)] = tok
            return tok

        def emit_ln_apply(cnk, tiles):
            tev = None
            for tt in tiles:
                row0 = cnk * CH + tt * 128
                rc = sb["rstdT"][:, tt:tt + 1]
                nm = sb["nmrT"][:, tt:tt + 1]
                tacts = []
                for qd in range(4):
                    slq = qd % 2
                    xqs = sb["xq"][:, slq * 1024:(slq + 1) * 1024]
                    tql = p.op("sync", lambda h, row0=row0, qd=qd, xqs=xqs: h.dma_start(
                        out=xqs, in_=D["x"][row0:row0 + 128, qd * 1024:(qd + 1) * 1024]),
                        waits=[ln["xq_guard"][slq], ln["xq_prev"][slq]], sem=XQ[slq], dma=True)
                    ln["xq_prev"][slq] = tql
                    tact = p.op("scalar", lambda h, qd=qd, xqs=xqs, rc=rc, nm=nm: h.activation(
                        sb["xnb"][:, qd * 1024:(qd + 1) * 1024], xqs, AF.Identity,
                        bias=nm, scale=rc),
                        waits=[tql, ln["xnb_guard"], ln[("nmr", cnk, tt)]] if qd == 0 else [tql])
                    tacts.append(tact)
                    ln["xq_guard"][slq] = tact
                for i in range(32):
                    pj = i % 2
                    ttr = p.op("tensor", lambda h, i=i, pj=pj: h.transpose(
                        P[pj].ap().bitcast(BF16)[:, 0:128],
                        sb["xnb"][:, i * 128:(i + 1) * 128], sb["ident_bf"][:]),
                        waits=[tacts[i // 8], tid_bf, psA[pj]], sem=TM)
                    tev = p.op("scalar", lambda h, i=i, tt=tt, pj=pj: h.copy(
                        sb["xnT"][:, i * 512 + tt * 128: i * 512 + (tt + 1) * 128],
                        P[pj].ap().bitcast(BF16)[:, 0:128]), waits=[ttr])
                    psA[pj] = tev
                    if i == 31:
                        ln["xnb_guard"] = ttr
            return tev

        tload = {}
        for name in ["Lam", "pri", "qri", "Vc", "Ct", "Bp", "Dsc", "lstep",
                     "c_re", "c_im", "Es1re", "Es1im", "Es3", "tw_re", "tw_im",
                     "Rrev", "ident_f", "ones_row", "exp_lo", "exp_h", "exp_e",
                     "gs", "us", "ds"]:
            if name == "ds":
                continue
            dst = {"gs": "gsl", "us": "usl"}.get(name, name)
            if dst in ("exp_lo", "exp_h", "exp_e", "gsl", "usl", "Vc", "ident_f"):
                tload[name] = p.op("sync", lambda h, n=name, d=dst: h.dma_start(out=sb[d], in_=D[n]),
                                   sem=LD, dma=True)
            else:
                tload[name] = p.op("sync", lambda h, n=name, d=dst: h.dma_start(out=sb[d][:], in_=D[n]),
                                   sem=LD, dma=True)
        tload["tS"] = p.op("sync", lambda h: h.dma_start(out=A_tS, in_=D["tS"]), sem=LD, dma=True)
        ALL_LOADS = (LD, LD["n"])
        for k in list(tload):
            tload[k] = ALL_LOADS

        tid_bf = p.op("vector", lambda h: h.tensor_copy(sb["ident_bf"][:], sb["ident_f"]),
                      waits=[tload["ident_f"]])
        trv_bf = p.op("vector", lambda h: h.tensor_copy(sb["Rrev_bf"][:], sb["Rrev"][:]),
                      waits=[tload["Rrev"]])

        t1 = p.op("tensor", lambda h: h.transpose(P[0][:, 0:86], sb["gsl"], sb["ident_f"][0:86, 0:86]),
                  waits=[tload["gs"], tload["ident_f"]], sem=TM)
        t2 = p.op("tensor", lambda h: h.transpose(P[1][:, 0:86], sb["usl"], sb["ident_f"][0:86, 0:86]),
                  waits=[tload["us"]], sem=TM)
        t3 = p.op("scalar", lambda h: h.copy(sb["s2"][:, 0:86], P[0][:, 0:86]), waits=[t1])
        t4 = p.op("scalar", lambda h: h.copy(sb["s2b"][:, 0:86], P[1][:, 0:86]), waits=[t2])
        ts2 = p.op("vector", lambda h: h.tensor_tensor(sb["s2"][:, 0:86], sb["s2"][:, 0:86],
                                                       sb["s2b"][:, 0:86], ALU.mult), waits=[t3, t4])
        psA[0], psA[1] = t3, t4
        for _tt in range(4):
            emit_ln_stats(0, [_tt])
            t_ln_done0 = emit_ln_apply(0, [_tt])

        dsrow = gsbf[0:1, 0:2048]
        tsbc = {}
        tdsh = {}
        for half in range(2):
            wds = [tload["ones_row"]] if half == 0 else [tsbc[3]]
            tdsh[half] = p.op("gpsimd", lambda h, half=half: h.dma_start(
                out=dsrow, in_=D["ds"][0:1, half * 2048:(half + 1) * 2048]),
                waits=wds, sem=KP, dma=True)
            for cc in range(4):
                c = half * 4 + cc
                tm = p.op("tensor", lambda h, c=c, cc=cc: h.matmul(P[2 + (c % 2)][:], sb["ones_row"][:],
                          dsrow[0:1, cc * 512:(cc + 1) * 512], start=True, stop=True),
                          waits=[tdsh[half]] + ([tsbc[c - 2]] if c >= 2 else []), sem=TM)
                tsbc[c] = p.op("scalar", lambda h, c=c: h.copy(sb["s_bc"][:, c * 512:(c + 1) * 512],
                                                               P[2 + (c % 2)][:]), waits=[tm])
        t_sbc_done = tsbc[7]
        tds_kp = (KP, KP["n"])

        ex1, ex2, ex3 = A_exp[0:1, 0:76], A_exp[0:1, 128:204], A_exp[0:1, 256:332]
        s11, s12, s13 = A_exp[0:1, 400:401], A_exp[0:1, 432:433], A_exp[0:1, 464:465]
        sstep, snrt = A_exp[0:1, 496:497], A_exp[0:1, 528:529]
        xs = sb["lstep"][0:1, 0:1]
        wv0 = [tload["lstep"], tload["exp_lo"], tload["exp_h"], tload["exp_e"]]
        p.op("vector", lambda h: h.tensor_scalar(ex1, sb["exp_lo"], xs, None, ALU.is_le), waits=wv0)
        p.op("vector", lambda h: h.tensor_scalar(ex2, sb["exp_lo"], 0.0625, xs, ALU.add, ALU.is_gt))
        p.op("vector", lambda h: h.tensor_tensor(ex1, ex1, ex2, ALU.mult))
        p.op("vector", lambda h: h.tensor_tensor(ex2, ex1, sb["exp_h"], ALU.mult))
        p.op("vector", lambda h: h.tensor_reduce(s12, ex2, AX.X, ALU.add))
        p.op("vector", lambda h: h.tensor_tensor(ex3, ex1, sb["exp_e"], ALU.mult))
        p.op("vector", lambda h: h.tensor_reduce(s13, ex3, AX.X, ALU.add))
        p.op("vector", lambda h: h.tensor_tensor(s11, xs, s12, ALU.subtract))
        p.op("vector", lambda h: h.tensor_scalar(s12, s11, 1.0 / 3.0, 1.0, ALU.mult, ALU.add))
        p.op("vector", lambda h: h.tensor_tensor(s12, s11, s12, ALU.mult))
        p.op("vector", lambda h: h.tensor_scalar(s12, s12, 0.5, 1.0, ALU.mult, ALU.add))
        p.op("vector", lambda h: h.tensor_tensor(s12, s11, s12, ALU.mult))
        p.op("vector", lambda h: h.tensor_scalar(s12, s12, 1.0, 1.0, ALU.mult, ALU.add))
        p.op("vector", lambda h: h.tensor_tensor(sstep, s13, s12, ALU.mult))
        recip_nr(sb["step2"][:], sstep, snrt)
        t3v = p.op("vector", lambda h: h.tensor_scalar(sb["step2"][:], sb["step2"][:], 2.0, None, ALU.mult))
        t4m = p.op("tensor", lambda h: h.matmul(P[4][:, 0:1], sb["ones_row"][0:1, :], sb["step2"][:],
                                                start=True, stop=True), waits=[t3v, tload["ones_row"]], sem=TM)
        t_g2 = p.op("scalar", lambda h: h.copy(sb["g2x"][:], P[4][:, 0:1]), waits=[t4m])
        p.op("vector", lambda h: h.tensor_copy(sb["lamS"][0:64, :], sb["Lam"][:, 0:1]), waits=[tload["Lam"]])
        t_lamS = p.op("vector", lambda h: h.tensor_copy(sb["lamS"][64:128, :], sb["Lam"][:, 1:2]))

        A_vtre, A_vtim = A_vt[:, 0:64], A_vt[:, 64:128]
        t5 = p.op("tensor", lambda h: h.transpose(P[5][0:64, 0:64], sb["Vc"][:, 0:128:2], sb["ident_f"][0:64, 0:64]),
                  waits=[tload["Vc"], tload["ident_f"]], sem=TM)
        t6 = p.op("tensor", lambda h: h.transpose(P[6][0:64, 0:64], sb["Vc"][:, 1:128:2], sb["ident_f"][0:64, 0:64]),
                  sem=TM)
        t7 = p.op("scalar", lambda h: h.copy(A_vtre, P[5][0:64, 0:64]), waits=[t5])
        t8 = p.op("scalar", lambda h: h.copy(A_vtim, P[6][0:64, 0:64]), waits=[t6])
        t9 = p.op("tensor", lambda h: h.matmul(P[5][0:64, 0:1], A_vtre, sb["Bp"][:], start=True, stop=True),
                  waits=[t7, tload["Bp"]], sem=TM)
        t10 = p.op("tensor", lambda h: h.matmul(P[6][0:64, 0:1], A_vtim, sb["Bp"][:], start=True, stop=True),
                   waits=[t8], sem=TM)
        B_re, B_im = A_B[:, 0:1], A_B[:, 1:2]
        tBr = p.op("scalar", lambda h: h.copy(B_re, P[5][0:64, 0:1]), waits=[t9])
        tBi = p.op("scalar", lambda h: h.copy(B_im, P[6][0:64, 0:1]), waits=[t10])

        ct_r, ct_i = sb["Ct"][:, 0:1], sb["Ct"][:, 1:2]
        q_r, q_i = sb["qri"][:, 0:1], sb["qri"][:, 1:2]
        p_r, p_i = sb["pri"][:, 0:1], sb["pri"][:, 1:2]
        tm1c, tm2c = mids["tm1"][:, 0:1], mids["tm2"][:, 0:1]
        LT = sb["lhsTS"]
        tv = [tBr, tBi, tload["Ct"], tload["qri"], tload["pri"]]
        for ar, ai, br, bi, col in [(ct_r, ct_i, B_re, B_im, 0), (ct_r, ct_i, p_r, p_i, 1),
                                    (q_r, q_i, B_re, B_im, 2), (q_r, q_i, p_r, p_i, 3)]:
            p.op("vector", lambda h, a=ar, b=br: h.tensor_tensor(tm1c, a, b, ALU.mult), waits=tv)
            p.op("vector", lambda h, a=ai, b=bi: h.tensor_tensor(tm2c, a, b, ALU.mult))
            p.op("vector", lambda h, cx=col: h.tensor_tensor(LT[0:64, cx:cx + 1], tm1c, tm2c, ALU.add))
            p.op("vector", lambda h, a=ar, b=bi: h.tensor_tensor(tm1c, a, b, ALU.mult))
            p.op("vector", lambda h, a=ai, b=br: h.tensor_tensor(tm2c, a, b, ALU.mult))
            p.op("vector", lambda h, cx=col: h.tensor_tensor(LT[0:64, 4 + cx:5 + cx], tm1c, tm2c, ALU.subtract))
            p.op("vector", lambda h, cx=col: h.tensor_copy(LT[64:128, cx:cx + 1], LT[0:64, 4 + cx:5 + cx]))
            p.op("vector", lambda h, cx=col: h.tensor_scalar(LT[64:128, 4 + cx:5 + cx], LT[0:64, cx:cx + 1],
                                                             -1.0, None, ALU.mult))
        tlh = p.op("vector", lambda h: h.tensor_copy(tm1c, tm1c))

        tneg = p.op("vector", lambda h: h.tensor_scalar(sb["neglam"][:], sb["lamS"][:], -1.0, None, ALU.mult),
                    waits=[t_lamS])
        tds = p.op("scalar", lambda h: h.activation(A_dS, A_tS, AF.Identity,
                                                    bias=sb["neglam"][:], scale=sb["g2x"][:]),
                   waits=[t_g2, tneg, tload["tS"]])
        tsqA = p.op("scalar", lambda h: h.activation(A_tS[0:64, :], A_dS[0:64, :], AF.Square), waits=[tds])
        tsqB = p.op("vector", lambda h: h.tensor_tensor(A_mag, A_dS[64:128, :], A_dS[64:128, :], ALU.mult),
                    waits=[tds])
        tsum = p.op("vector", lambda h: h.tensor_tensor(A_tS[0:64, :], A_tS[0:64, :], A_mag, ALU.add),
                    waits=[tsqA])
        trc = p.op("vector", lambda h: h.reciprocal(A_mag, A_tS[0:64, :]))
        tcp = p.op("scalar", lambda h: h.copy(A_tS[64:128, :], A_mag), waits=[trc])
        p.op("vector", lambda h: h.tensor_tensor(A_dS[0:64, :], A_dS[0:64, :], A_mag, ALU.mult), waits=[trc])
        tinv = p.op("vector", lambda h: h.tensor_tensor(A_dS[64:128, :], A_dS[64:128, :],
                                                        A_tS[64:128, :], ALU.mult), waits=[tcp])

        tkabev = {}
        for kc in range(8):
            w = [tlh, tinv] + ([tkabev[kc - 2]] if kc >= 2 else [psA[0], psA[1]])
            tmm = p.op("tensor", lambda h, kc=kc: h.matmul(P[kc % 2][0:8, :], LT[:],
                       A_dS[:, kc * 512:(kc + 1) * 512], start=True, stop=True), waits=w, sem=TM)
            tkabev[kc] = p.op("scalar", lambda h, kc=kc: h.copy(A_kab[:, kc * 512:(kc + 1) * 512],
                                                                P[kc % 2][0:8, :]), waits=[tmm])
        tkh = p.op("gpsimd", lambda h: h.dma_start(out=kabh.rearrange("r p f -> r (p f)"), in_=A_kab),
                   waits=[tkabev[kc] for kc in range(8)], sem=KP, dma=True)
        for r, nm in enumerate(fan_names):
            p.op("gpsimd", lambda h, r=r, nm=nm: h.dma_start(out=fanmap[nm], in_=kabh[r]),
                 waits=[tkh], sem=KP, dma=True)
        tfan = [(KP, KP["n"])]

        k00r, k01r, k10r, k11r = fanmap["k00r"], fanmap["k01r"], fanmap["k10r"], fanmap["k11r"]
        k00i, k01i, k10i, k11i = fanmap["k00i"], fanmap["k01i"], fanmap["k10i"], fanmap["k11i"]
        w_re, w_imP = mids["w_re"], mids["w_imP"]
        n_re, n_im = mids["n_re"], mids["n_im"]
        s_re, s_im = mids["s_re"], mids["s_im"]
        tm1f, tm2f, tm3f = mids["tm1"], mids["tm2"], mids["tm3"]
        p.op("vector", lambda h: h.tensor_scalar(tm1f, k11r, 1.0, None, ALU.add),
             waits=tfan + [tload["c_re"], tload["c_im"]])
        p.op("vector", lambda h: h.tensor_tensor(tm2f, tm1f, tm1f, ALU.mult))
        p.op("vector", lambda h: h.tensor_tensor(tm3f, k11i, k11i, ALU.mult))
        p.op("vector", lambda h: h.tensor_tensor(tm2f, tm2f, tm3f, ALU.add))
        recip_nr(w_imP, tm2f, tm3f)
        p.op("vector", lambda h: h.tensor_copy(tm2f, w_imP))
        p.op("vector", lambda h: h.tensor_tensor(w_re, tm1f, tm2f, ALU.mult))
        p.op("vector", lambda h: h.tensor_tensor(w_imP, k11i, tm2f, ALU.mult))
        p.op("vector", lambda h: h.tensor_tensor(tm1f, k01r, k10r, ALU.mult))
        p.op("vector", lambda h: h.tensor_tensor(tm2f, k01i, k10i, ALU.mult))
        p.op("vector", lambda h: h.tensor_tensor(n_re, tm1f, tm2f, ALU.subtract))
        p.op("vector", lambda h: h.tensor_tensor(tm1f, k01r, k10i, ALU.mult))
        p.op("vector", lambda h: h.tensor_tensor(tm2f, k01i, k10r, ALU.mult))
        p.op("vector", lambda h: h.tensor_tensor(n_im, tm1f, tm2f, ALU.add))
        p.op("vector", lambda h: h.tensor_tensor(tm1f, n_re, w_re, ALU.mult))
        p.op("vector", lambda h: h.tensor_tensor(tm2f, n_im, w_imP, ALU.mult))
        p.op("vector", lambda h: h.tensor_tensor(tm3f, tm1f, tm2f, ALU.add))
        p.op("vector", lambda h: h.tensor_tensor(s_re, k00r, tm3f, ALU.subtract))
        p.op("vector", lambda h: h.tensor_tensor(tm1f, n_im, w_re, ALU.mult))
        p.op("vector", lambda h: h.tensor_tensor(tm2f, n_re, w_imP, ALU.mult))
        p.op("vector", lambda h: h.tensor_tensor(tm3f, tm1f, tm2f, ALU.subtract))
        p.op("vector", lambda h: h.tensor_tensor(s_im, k00i, tm3f, ALU.subtract))
        p.op("vector", lambda h: h.tensor_tensor(tm1f, sb["c_re"][:], s_re, ALU.mult))
        p.op("vector", lambda h: h.tensor_tensor(tm2f, sb["c_im"][:], s_im, ALU.mult))
        p.op("vector", lambda h: h.tensor_tensor(A_aS[0:64, :], tm1f, tm2f, ALU.subtract))
        p.op("vector", lambda h: h.tensor_tensor(tm1f, sb["c_re"][:], s_im, ALU.mult))
        p.op("vector", lambda h: h.tensor_tensor(tm2f, sb["c_im"][:], s_re, ALU.mult))
        tat = p.op("vector", lambda h: h.tensor_tensor(A_aS[64:128, :], tm1f, tm2f, ALU.add))

        ts1 = p.op("tensor", lambda h: h.matmul(P[2][0:64, 0:64], A_aS, sb["Es1re"][:], start=True, stop=True),
                   waits=[tat, tload["Es1re"], tload["Es1im"], t_sbc_done], sem=TM)
        ts2m = p.op("tensor", lambda h: h.matmul(P[3][0:64, 0:64], A_aS, sb["Es1im"][:], start=True, stop=True),
                    sem=TM)
        st_re, st_im = mids["st_re"], mids["st_im"]
        te1 = p.op("scalar", lambda h: h.copy(st_re, P[2][0:64, 0:64]), waits=[ts1])
        te2 = p.op("scalar", lambda h: h.copy(st_im, P[3][0:64, 0:64]), waits=[ts2m])
        p.op("vector", lambda h: h.tensor_tensor(tm1f, st_re, sb["tw_re"][:], ALU.mult),
             waits=[te1, te2, tload["tw_re"], tload["tw_im"]])
        p.op("vector", lambda h: h.tensor_tensor(tm2f, st_im, sb["tw_im"][:], ALU.mult))
        p.op("vector", lambda h: h.tensor_tensor(A_TS[0:64, :], tm1f, tm2f, ALU.subtract))
        p.op("vector", lambda h: h.tensor_tensor(tm1f, st_re, sb["tw_im"][:], ALU.mult))
        p.op("vector", lambda h: h.tensor_tensor(tm2f, st_im, sb["tw_re"][:], ALU.mult))
        ttw = p.op("vector", lambda h: h.tensor_tensor(A_TS[64:128, :], tm1f, tm2f, ALU.add))
        ts3 = p.op("tensor", lambda h: h.matmul(P[2][0:64, 0:64], sb["Es3"][:], A_TS, start=True, stop=True),
                   waits=[ttw, tload["Es3"], te1], sem=TM)
        tk1 = p.op("scalar", lambda h: h.copy(sb["Ksb"], P[2][0:64, 0:64]), waits=[ts3])
        tkD = p.op("scalar", lambda h: h.activation(sb["Ksb"][0:1, 0:1], sb["Dsc"][:], AF.Identity,
                                                    bias=sb["Ksb"][0:1, 0:1], scale=1.0),
                   waits=[tk1, tload["Dsc"]])
        tz = p.op("vector", lambda h: h.memset(sb["tmp"][0:1, 0:PAD], 0.0))
        tz1 = p.op("gpsimd", lambda h: h.dma_start(out=kpad[0:PAD].rearrange("(a b) -> a b", a=1),
                                                   in_=sb["tmp"][0:1, 0:PAD]), waits=[tz], sem=KZ, dma=True)
        tz2 = p.op("gpsimd", lambda h: h.dma_start(out=kpad[PAD:PAD + L].rearrange("(a b) -> a b", a=64),
                                                   in_=sb["Ksb"]), waits=[tkD], sem=KZ, dma=True)
        tkz_all = (KZ, KZ["n"])

        psA[0], psA[1], psA[2], psA[3] = tkabev[6], tkabev[7], tk1, te2
        psB = [t_g2, tBr, tBi, None]
        G46 = [tat, ttw, tinv, ts1, ts2m, ts3, t9, t10, t1, t2, ALL_LOADS]
        slot_guards = {0: [tkh], 1: [tkh], 2: [tkh], 3: [tkh],
                       4: G46, 5: G46, 6: G46,
                       7: [t5, t6, ts2, t4, tkz_all, ALL_LOADS], 8: []}
        slot_first = [True] * NSLOT
        t_sh_store = None

        slot_free = [None] * NSLOT
        slot_prev = [None] * NSLOT
        dslot_free = [None, None]
        dslot_prev = [None, None]
        win_prev = [None] * 4
        xp_prev = [None, None]
        xp_guard = [None, None]
        tmp_guard = [tkz_all, tkz_all]

        ssm_sb_free = [[None] * 4, [None] * 4]
        gsb_free = [t_sbc_done] * 4
        prev_chunk_ssm_done = None
        prev_strip_load = None
        prev_strip_ev = None
        gsb_prev = [None] * 4
        ln_apply_tok = {}
        last_gu_read = None
        t_arena_free = None
        slot_idx = 0
        t_out_all = []

        for cnk in range(NCH):
            t_ln_done = t_ln_done0 if cnk == 0 else ln_apply_tok[cnk]

            tok_gu_w = {}
            arena_guard = [prev_chunk_ssm_done, prev_strip_load]
            for gi, (ib0, gw) in enumerate(IGROUPS):
                for mat in ("g", "u"):
                    psum_set = PA if mat == "g" else PB
                    pf = psA if mat == "g" else psB
                    tok_piece = {}
                    slot_of = {}
                    for q in range(8):
                        sl = slot_idx % NSLOT
                        slot_idx += 1
                        wpd = [slot_free[sl], slot_prev[sl]] + arena_guard
                        if cnk == 0 and slot_first[sl]:
                            wpd += slot_guards[sl]
                            slot_first[sl] = False
                        src_ap = (D[mat + "main"][gi, q] if gw == 4 else D[mat + "tail"][q])
                        dst = sb["arena"][:, sl * 2048: sl * 2048 + 4 * gw * 128]
                        tok_piece[q] = p.op("gpsimd", lambda h, src_ap=src_ap, dst=dst: h.dma_start(
                            out=dst, in_=src_ap.rearrange("p hb i -> p (hb i)")),
                            waits=wpd, sem=WSL[sl], dma=True)
                        slot_prev[sl] = tok_piece[q]
                        slot_of[q] = sl
                    for j in range(gw):
                        tbank = None
                        for hb in range(32):
                            q = hb // 4
                            sl = slot_of[q]
                            give_tok = (hb == 31) or (j == gw - 1 and hb % 4 == 3)
                            wmm = []
                            if hb == 0:
                                wmm = [pf[j]]
                                if j == 0:
                                    wmm += [tok_piece[0], t_ln_done]
                            elif hb % 4 == 0 and j == 0:
                                wmm = [tok_piece[q]]
                            tmm = p.op("tensor", lambda h, sl=sl, hb=hb, j=j, gw=gw, pj=psum_set[j]: h.matmul(
                                pj[:],
                                sb["arena"][:, sl * 2048 + (hb % 4) * (gw * 128) + j * 128:
                                            sl * 2048 + (hb % 4) * (gw * 128) + (j + 1) * 128],
                                sb["xnT"][:, hb * 512:(hb + 1) * 512],
                                start=(hb == 0), stop=(hb == 31)),
                                waits=wmm, sem=TM if give_tok else None)
                            if give_tok:
                                if hb == 31:
                                    tbank = tmm
                                if j == gw - 1 and hb % 4 == 3:
                                    slot_free[sl] = tmm
                        if mat == "g":
                            tgc = p.op("scalar", lambda h, j=j: h.copy(g_sb[j], PA[j][:]),
                                       waits=[tbank, gsb_free[j], dslot_prev[j // 2], dslot_free[j // 2], tds_kp])
                            gsb_prev[j] = tgc
                            psA[j] = tgc
                            tok_gu_w[("gsb", gi, j)] = tgc
                        else:
                            ib = ib0 + j
                            wst = [tbank, tok_gu_w[("gsb", gi, j)], ts2]
                            if cnk == 0 and ib < 64:
                                wst.append(tkabev[7])
                            tst = p.op("vector", lambda h, ib=ib, j=j: h.scalar_tensor_tensor(
                                sb["gu"][:, ib * 512:(ib + 1) * 512], PB[j][:],
                                sb["s2"][:, ib:ib + 1], g_sb[j], ALU.mult, ALU.mult), waits=wst)
                            psB[j] = tst
                            gsb_free[j] = tst
                            tok_gu_w[ib] = tst
            t_gu_last = tok_gu_w[NIB - 1]
            if cnk + 1 < NCH:
                emit_ln_stats(cnk + 1, [0, 1, 2, 3])

            _strips_state = {}

            def emit_strips_now():
                nonlocal prev_strip_load
                if cnk == 0:
                    t_strip0 = {}
                    win_prev0 = [None] * 4

                    def win_dma0(h, off, wb):
                        w = kpad.copy()
                        w.offset = off
                        w.ap[0] = [1, 128]
                        w.ap.append([1, 512])
                        return h.dma_start(out=wb, in_=w)
                    for s, dd in enumerate(range(-3, 29)):
                        off = PAD - 127 + 128 * dd
                        wb = wbufB[s % 4]
                        wwin = [tkz_all, t_gu_last, win_prev0[s % 4]] + ([t_strip0[s - 4]] if s >= 4 else [])
                        twn = p.op("gpsimd", lambda h, off=off, wb=wb: win_dma0(h, off, wb),
                                   waits=wwin, sem=WIN[s % 4], dma=True)
                        win_prev0[s % 4] = twn
                        wrm = [twn, trv_bf, psA[s % 2], t_gu_last]
                        tsm = p.op("tensor", lambda h, s=s, wb=wb: h.matmul(P[s % 2][:], sb["Rrev_bf"][:], wb,
                                   start=True, stop=True), waits=wrm, sem=TM)
                        t_strip0[s] = p.op("scalar", lambda h, s=s: h.copy(
                            sb["arena"][:, s * 512:(s + 1) * 512], P[s % 2][:]),
                            waits=[tsm, slot_prev[s // 4]])
                        psA[s % 2] = t_strip0[s]
                    t_sh_store = p.op("gpsimd", lambda h: h.dma_start(out=striph, in_=sb["arena"][:, 0:16384]),
                                      waits=[t_strip0[s] for s in range(32)], sem=SH, dma=True)


                    t_sload = t_strip0[31]
                else:
                    t_sload = p.op("gpsimd", lambda h: h.dma_start(out=sb["arena"][:, 0:16384], in_=striph),
                                   waits=[t_gu_last, prev_strip_load] + [slot_prev[i] for i in range(NSLOT)]
                                         + [slot_free[i] for i in range(NSLOT)],
                                   sem=SH, dma=True)
                prev_strip_load = t_sh_store if cnk == 0 else t_sload
                for i in range(8):
                    slot_prev[i] = prev_strip_load

                _strips_state["t_sload"] = t_sload
            ssm_par = [sb["ssm0"], sb["ssm1"]]
            t_ssm_ev = {}
            last_ssm_mm = None
            last_down_mm = None
            for c in range(8):
                par = c % 2
                if c == 0:
                    txp_c = {}
                    for tb in range(2):
                        row0 = cnk * CH + tb * 128
                        sl2 = tb % 2
                        txp_c[tb] = p.op("scalar", lambda h, row0=row0, c=c, sl2=sl2: h.dma_start(
                            out=sb["xp"][:, sl2 * 512:(sl2 + 1) * 512],
                            in_=D["x"][row0:row0 + 128, c * 512:(c + 1) * 512]),
                            waits=[xp_guard[sl2], xp_prev[sl2]], sem=XP[sl2], dma=True)
                        xp_prev[sl2] = txp_c[tb]
                    for jj, (ib0, gw4) in enumerate(IGROUPS):
                        dsl_i = jj % 2
                        wdp = [dslot_free[dsl_i], dslot_prev[dsl_i]]
                        if c == 0 and jj < 2:
                            wdp += [gsb_free[j] for j in range(4)] + [gsb_prev[j] for j in range(4)]
                        src = (D["dmain"][c, jj] if gw4 == 4 else D["dtail"][c])
                        dstp = sb["dring"][:, dsl_i * 2048: dsl_i * 2048 + gw4 * 512]
                        tdp = p.op("sync", lambda h, src=src, dstp=dstp: h.dma_start(
                            out=dstp.bitcast(U16), in_=src.rearrange("p j hw -> p (j hw)")),
                            waits=wdp, sem=DSL[dsl_i], dma=True)
                        dslot_prev[dsl_i] = tdp
                        tgrp = None
                        for j in range(gw4):
                            ib = ib0 + j
                            for tb in range(4):
                                wmm = []
                                if j == 0 and tb == 0:
                                    wmm.append(tdp)
                                if ib == 0:
                                    wmm += [psB[tb], t_gu_last]
                                give = (j == gw4 - 1 and tb == 3)
                                tmm = p.op("tensor", lambda h, ib=ib, tb=tb, dsl_i=dsl_i, j=j: h.matmul(
                                    PB[tb][:],
                                    sb["gu"][:, ib * 512 + tb * 128: ib * 512 + (tb + 1) * 128],
                                    sb["dring"][:, dsl_i * 2048 + j * 512: dsl_i * 2048 + (j + 1) * 512],
                                    start=(ib == 0), stop=(ib == NIB - 1)),
                                    waits=wmm, sem=TM if give else None)
                                if give:
                                    tgrp = tmm
                        dslot_free[dsl_i] = tgrp
                        if ib0 + gw4 == NIB:
                            last_down_mm = tgrp
                    emit_strips_now()
                Ilist = list(range(0, min(32, 4 * c + 4)))
                for tb in range(4):
                    tmm = None
                    for ix, I in enumerate(Ilist):
                        sidx = 4 * c - I + 3
                        last = (ix == len(Ilist) - 1)
                        wss = []
                        if ix == 0:
                            wss = [_strips_state["t_sload"], psA[tb]]
                        tmm = p.op("tensor", lambda h, I=I, tb=tb, sidx=sidx, first=(ix == 0), last=last:
                                   h.matmul(PA[tb][:],
                                            sb["xnT"][:, I * 512 + tb * 128: I * 512 + (tb + 1) * 128],
                                            sb["arena"][:, sidx * 512:(sidx + 1) * 512],
                                            start=first, stop=last),
                                   waits=wss, sem=TM if last else None)
                    last_ssm_mm = tmm
                    tse = p.op("scalar", lambda h, par=par, tb=tb: h.copy(
                        ssm_par[par][:, tb * 512:(tb + 1) * 512], PA[tb][:]),
                        waits=[tmm, ssm_sb_free[par][tb]])
                    t_ssm_ev[(c, tb)] = tse
                    psA[tb] = tse
                if c == 7 and cnk + 1 < NCH:
                    ln_apply_tok[cnk + 1] = emit_ln_apply(cnk + 1, [0, 1, 2, 3])
                if c > 0:
                    txp_c = {}
                    for tb in range(2):
                        row0 = cnk * CH + tb * 128
                        sl2 = tb % 2
                        txp_c[tb] = p.op("scalar", lambda h, row0=row0, c=c, sl2=sl2: h.dma_start(
                            out=sb["xp"][:, sl2 * 512:(sl2 + 1) * 512],
                            in_=D["x"][row0:row0 + 128, c * 512:(c + 1) * 512]),
                            waits=[xp_guard[sl2], xp_prev[sl2]], sem=XP[sl2], dma=True)
                        xp_prev[sl2] = txp_c[tb]
                    for jj, (ib0, gw4) in enumerate(IGROUPS):
                        dsl_i = jj % 2
                        wdp = [dslot_free[dsl_i], dslot_prev[dsl_i]]
                        if c == 0 and jj < 2:
                            wdp += [gsb_free[j] for j in range(4)] + [gsb_prev[j] for j in range(4)]
                        src = (D["dmain"][c, jj] if gw4 == 4 else D["dtail"][c])
                        dstp = sb["dring"][:, dsl_i * 2048: dsl_i * 2048 + gw4 * 512]
                        tdp = p.op("sync", lambda h, src=src, dstp=dstp: h.dma_start(
                            out=dstp.bitcast(U16), in_=src.rearrange("p j hw -> p (j hw)")),
                            waits=wdp, sem=DSL[dsl_i], dma=True)
                        dslot_prev[dsl_i] = tdp
                        tgrp = None
                        for j in range(gw4):
                            ib = ib0 + j
                            for tb in range(4):
                                wmm = []
                                if j == 0 and tb == 0:
                                    wmm.append(tdp)
                                if ib == 0:
                                    wmm += [psB[tb], t_gu_last]
                                give = (j == gw4 - 1 and tb == 3)
                                tmm = p.op("tensor", lambda h, ib=ib, tb=tb, dsl_i=dsl_i, j=j: h.matmul(
                                    PB[tb][:],
                                    sb["gu"][:, ib * 512 + tb * 128: ib * 512 + (tb + 1) * 128],
                                    sb["dring"][:, dsl_i * 2048 + j * 512: dsl_i * 2048 + (j + 1) * 512],
                                    start=(ib == 0), stop=(ib == NIB - 1)),
                                    waits=wmm, sem=TM if give else None)
                                if give:
                                    tgrp = tmm
                        dslot_free[dsl_i] = tgrp
                        if ib0 + gw4 == NIB:
                            last_down_mm = tgrp
                for tb in range(4):
                    row0 = cnk * CH + tb * 128
                    sl2 = tb % 2
                    if tb >= 2:
                        txp_c[tb] = p.op("scalar", lambda h, row0=row0, c=c, sl2=sl2: h.dma_start(
                            out=sb["xp"][:, sl2 * 512:(sl2 + 1) * 512],
                            in_=D["x"][row0:row0 + 128, c * 512:(c + 1) * 512]),
                            waits=[xp_guard[sl2], xp_prev[sl2]], sem=XP[sl2], dma=True)
                        xp_prev[sl2] = txp_c[tb]
                    txp = txp_c[tb]
                    tv1 = p.op("vector", lambda h, tb=tb, c=c, sl2=sl2: h.tensor_tensor(
                        sb["tmp"][:, sl2 * 512:(sl2 + 1) * 512], PB[tb][:],
                        sb["s_bc"][:, c * 512:(c + 1) * 512], ALU.mult),
                        waits=[last_down_mm, t_sbc_done, tmp_guard[sl2]])
                    tv2 = p.op("vector", lambda h, tb=tb, par=par, sl2=sl2: h.tensor_tensor(
                        sb["tmp"][:, sl2 * 512:(sl2 + 1) * 512],
                        sb["tmp"][:, sl2 * 512:(sl2 + 1) * 512],
                        ssm_par[par][:, tb * 512:(tb + 1) * 512], ALU.add),
                        waits=[t_ssm_ev[(c, tb)]])
                    tv3 = p.op("vector", lambda h, tb=tb, sl2=sl2: h.tensor_tensor(
                        sb["tmp"][:, sl2 * 512:(sl2 + 1) * 512],
                        sb["tmp"][:, sl2 * 512:(sl2 + 1) * 512],
                        sb["xp"][:, sl2 * 512:(sl2 + 1) * 512], ALU.add),
                        waits=[txp])
                    psB[tb] = tv1
                    ssm_sb_free[par][tb] = tv2
                    xp_guard[sl2] = tv3
                    tst_o = p.op("sync", lambda h, row0=row0, c=c, sl2=sl2: h.dma_start(
                        out=OUT[row0:row0 + 128, c * 512:(c + 1) * 512],
                        in_=sb["tmp"][:, sl2 * 512:(sl2 + 1) * 512]),
                        waits=[tv3], sem=STS[sl2], dma=True)
                    tmp_guard[sl2] = tst_o
                    t_out_all.append(tst_o)
            prev_chunk_ssm_done = last_ssm_mm
            last_gu_read = last_down_mm
            t_arena_free = last_ssm_mm

        p.op("sync", lambda h: h.nop(), waits=[(STS[0], STS[0]["n"]), (STS[1], STS[1]["n"])])

        p.replay(block)
    return nc


def kernel(L0_attn_u, L0_attn_Lambda_ri, L0_attn_p_ri, L0_attn_q_ri, L0_attn_Vc_ri,
           L0_attn_Ct_ri, L0_attn_Bp, L0_attn_D, L0_attn_log_step,
           L0_ffn_gate_wq, L0_ffn_gate_scale, L0_ffn_up_wq, L0_ffn_up_scale,
           L0_ffn_down_wq, L0_ffn_down_scale):
    tbl = host_tables()
    wst = stage_weights(np.asarray(L0_ffn_gate_wq), np.asarray(L0_ffn_up_wq),
                        np.asarray(L0_ffn_down_wq))
    xfull = np.ascontiguousarray(np.asarray(L0_attn_u, dtype=np.float32).reshape(8192, L))

    common = dict(
        Lam=np.ascontiguousarray(np.asarray(L0_attn_Lambda_ri, np.float32)),
        pri=np.ascontiguousarray(np.asarray(L0_attn_p_ri, np.float32)),
        qri=np.ascontiguousarray(np.asarray(L0_attn_q_ri, np.float32)),
        Vc=np.ascontiguousarray(np.asarray(L0_attn_Vc_ri, np.float32).reshape(64, 128)),
        Ct=np.ascontiguousarray(np.asarray(L0_attn_Ct_ri, np.float32).reshape(64, 2)),
        Bp=np.ascontiguousarray(np.asarray(L0_attn_Bp, np.float32)),
        Dsc=np.ascontiguousarray(np.asarray(L0_attn_D, np.float32).reshape(1, 1)),
        lstep=np.ascontiguousarray(np.asarray(L0_attn_log_step, np.float32).reshape(1, 1)),
        gs=np.ascontiguousarray(np.asarray(L0_ffn_gate_scale, np.float32).reshape(86, 128)),
        us=np.ascontiguousarray(np.asarray(L0_ffn_up_scale, np.float32).reshape(86, 128)),
        ds=np.ascontiguousarray(np.asarray(L0_ffn_down_scale, np.float32).reshape(1, L)),
        gmain=wst["gmain"], gtail=wst["gtail"],
        umain=wst["umain"], utail=wst["utail"],
        dmain=wst["dmain"], dtail=wst["dtail"],
        **tbl,
    )
    in_maps = []
    for c in range(NCORE):
        m = dict(common)
        m["x"] = np.ascontiguousarray(xfull[c * TOKC:(c + 1) * TOKC])
        in_maps.append(m)

    nc = build_program()
    res = run_bass_kernel_spmd(nc, in_maps, core_ids=list(range(NCORE)))
    out = np.concatenate([np.asarray(r["o"], np.float32) for r in res.results], 0)
    return out.reshape(2, 4096, L)
